# version 09
# baseline (speedup 1.0000x reference)
import numpy as np
import ml_dtypes
from contextlib import ExitStack

import concourse.bass as bass
import concourse.mybir as mybir
from concourse.bass_utils import run_bass_kernel_spmd

F32 = mybir.dt.float32
BF16 = mybir.dt.bfloat16
I32 = mybir.dt.int32
AF = mybir.ActivationFunctionType
ALU = mybir.AluOpType
AX = mybir.AxisListType

D = 1024
S = 4096
B = 4
NCORES = 8
TOK = 2048
NCH = 8
ROLE_BLOCKS = {0: (0, 3), 1: (1, 2)}

STAGE = 4



def _pm(vec, nch):
    return np.ascontiguousarray(np.asarray(vec).reshape(nch, 128).T)


def _tiles(WT):
    IN, OUT = WT.shape
    A = WT.reshape(IN // 128, 128, OUT // 128, 128).transpose(1, 2, 0, 3)
    return np.ascontiguousarray(A.reshape(128, (OUT // 128) * IN))


def stage_inputs(inp):
    x = np.asarray(inp["L0_attn_x"])
    WqkvT = np.asarray(inp["L0_attn_Wqkv"]).T
    WoutT = np.asarray(inp["L0_attn_Wout"]).T
    W_fcT = np.ascontiguousarray(np.asarray(inp["L0_ffn_W_fc"]).T)
    W_projT = np.ascontiguousarray(np.asarray(inp["L0_ffn_W_proj"]).T)
    W1T = np.ascontiguousarray(np.asarray(inp["L1_ffn_W1"]).T)
    W2T = np.ascontiguousarray(np.asarray(inp["L1_ffn_W2"]).T)
    shared = {
        "Wqkv_t": _tiles(WqkvT),
        "Wout_t": _tiles(WoutT),
        "bqkv": _pm(inp["L0_attn_bqkv"], 24),
        "bqkv_row": np.ascontiguousarray(np.asarray(inp["L0_attn_bqkv"])[None, 2048:3072]),
        "bout": _pm(inp["L0_attn_bout"], 8),
        "AfcT_t": np.ascontiguousarray(np.asarray(inp["L0_ffn_A_fc"]).T.reshape(8, 128, 16).transpose(1, 0, 2).reshape(128, 128)),
        "b_fc": _pm(inp["L0_ffn_b_fc"], 32),
        "b_proj": _pm(inp["L0_ffn_b_proj"], 8),
        "Wcd_t": _tiles(np.asarray(inp["L1_attn_Wc_down"])),
        "Wqu_t": _tiles(np.asarray(inp["L1_attn_Wq_up"])),
        "Wkv_t": _tiles(np.asarray(inp["L1_attn_Wkv_up"])),
        "Wcp_t": _tiles(np.asarray(inp["L1_attn_Wc_proj"])),
        "W1b": _pm(inp["L1_ffn_b1"], 32),
        "W2b": _pm(inp["L1_ffn_b2"], 8),
        "BfcT": np.ascontiguousarray(np.asarray(inp["L0_ffn_B_fc"]).T),
        "BprojT": np.ascontiguousarray(np.asarray(inp["L0_ffn_B_proj"]).T),
        "bproj_row": np.ascontiguousarray(np.asarray(inp["L0_ffn_b_proj"])[None, :]),
        "b2_row": np.ascontiguousarray(np.asarray(inp["L1_ffn_b2"])[None, :]),
        "bfc_row": np.ascontiguousarray(np.asarray(inp["L0_ffn_b_fc"])[None, :]),
        "b1_row": np.ascontiguousarray(np.asarray(inp["L1_ffn_b1"])[None, :]),
        "AprojT_t": np.ascontiguousarray(np.asarray(inp["L0_ffn_A_proj"]).T.reshape(32, 128, 16).transpose(1, 0, 2).reshape(128, 512)),
    }
    per_core = []
    for c in range(NCORES):
        s = c // 2
        role = c & 1
        blocks = ROLE_BLOCKS[role]
        xo = np.concatenate([x[s, 1024 * g:1024 * (g + 1), :] for g in blocks], axis=0)
        halos = []
        for g in blocks:
            lo = np.zeros((256, D), np.float32) if g == 0 else x[s, 1024 * g - 256:1024 * g, :]
            hi = np.zeros((256, D), np.float32) if g == 3 else x[s, 1024 * (g + 1):1024 * (g + 1) + 256, :]
            halos += [lo, hi]
        d = dict(shared)
        d["x_own"] = np.ascontiguousarray(xo.T)
        d["x_halo"] = np.ascontiguousarray(np.concatenate(halos, axis=0).T)
        d["role"] = np.array([[role]], np.int32)
        for nm, Wt, w in (("WfcTs", W_fcT, 512), ("W1Ts", W1T, 512)):
            sl = Wt[:, w * c:w * (c + 1)]
            d[nm] = np.ascontiguousarray(sl.reshape(8, 128, w).transpose(1, 0, 2).reshape(128, 8 * w))
        for nm, Wt, w in (("WprojTs", W_projT, 128), ("W2Ts", W2T, 128)):
            sl = Wt[:, w * c:w * (c + 1)]
            d[nm] = np.ascontiguousarray(sl.reshape(32, 128, w).transpose(1, 0, 2).reshape(128, 32 * w))
        per_core.append(d)
    return per_core


def gather_output(res):
    out = np.empty((B, S, D), np.float32)
    for c in range(NCORES):
        s = c // 2
        role = c & 1
        xo = np.asarray(res.results[c]["x_out"]).T
        for i, g in enumerate(ROLE_BLOCKS[role]):
            out[s, 1024 * g:1024 * (g + 1), :] = xo[1024 * i:1024 * (i + 1), :]
    return out



class Ctr:
    def __init__(self, sem, step=1):
        self.sem = sem
        self.step = step
        self.v = 0

    def bump(self, instr):
        instr.then_inc(self.sem, self.step)
        self.v += self.step
        return (self, self.v)

    def tok(self):
        return (self, self.v)


def waitall(eng, toks):
    for t in toks:
        if t is None:
            continue
        c, v = t
        if v > 0:
            eng.wait_ge(c.sem, v)


class P:
    pass


def build_program():
    nc = bass.Bass("TRN2", target_bir_lowering=False, debug=False)
    p = P()
    p.nc = nc

    io = {}

    def din(name, shape, dt=F32):
        io[name] = nc.dram_tensor(name, shape, dt, kind="ExternalInput").ap()

    din("x_own", [D, TOK]); din("x_halo", [D, 1024]); din("role", [1, 1], I32)
    din("Wqkv_t", [128, 24 * 1024]); din("bqkv", [128, 24]); din("bqkv_row", [1, 1024])
    din("Wout_t", [128, 8 * 1024]); din("bout", [128, 8])
    din("WfcTs", [128, 8 * 512]); din("WprojTs", [128, 32 * 128])
    din("W1Ts", [128, 8 * 512]); din("W2Ts", [128, 32 * 128])
    din("AfcT_t", [128, 128]); din("BfcT", [16, 4096])
    din("AprojT_t", [128, 512]); din("BprojT", [16, 1024])
    din("b_fc", [128, 32]); din("b_proj", [128, 8])
    din("bproj_row", [1, 1024]); din("b2_row", [1, 1024])
    din("bfc_row", [1, 4096]); din("b1_row", [1, 4096])
    din("Wcd_t", [128, 4 * 1024]); din("Wqu_t", [128, 8 * 512])
    din("Wkv_t", [128, 512]); din("Wcp_t", [128, 8 * 1024])
    din("W1b", [128, 32]); din("W2b", [128, 8])
    io["x_out"] = nc.dram_tensor("x_out", [D, TOK], F32, kind="ExternalOutput").ap()
    p.io = io

    p.cc_qin = nc.dram_tensor("cc_qin", [4], F32).ap()
    p.cc_qout = nc.dram_tensor("cc_qout", [32], F32).ap()
    p.cc_win = nc.dram_tensor("cc_win", [4 * 512 * 1024], BF16).ap()
    p.cc_wout = nc.dram_tensor("cc_wout", [8 * 4 * 512 * 1024], BF16).ap()
    p.cc_kin = nc.dram_tensor("cc_kin", [64 * TOK + TOK * 64], BF16).ap()
    p.cc_kout = nc.dram_tensor("cc_kout", [2 * (64 * TOK + TOK * 64)], BF16).ap()

    es = ExitStack()
    with es:
        sb = lambda name, shape, dt: es.enter_context(nc.sbuf_tensor(name, shape, dt))
        psf = lambda name, shape, dt: es.enter_context(nc.psum_tensor(name, shape, dt))

        p.xs = sb("xs", [128, NCH * TOK], F32)
        p.warena = sb("warena", [128, 32768], BF16)
        p.aarena = sb("aarena", [128, 29824], BF16)
        p.wsl = sb("wsl", [128, 512], F32)
        p.fsc = sb("fsc", [128, 1024], F32)
        p.vbias = sb("vbias", [1, 1024], BF16)
        p.small = sb("small", [128, 128], F32)
        p.onesr = sb("onesr", [128, 128], F32)
        p.consts = sb("consts", [128, 5380], BF16)
        p.biases = sb("biases", [128, 112], F32)
        p.rolei = sb("rolei", [1, 1], I32)

        p.psA = psf("psA", [128, 1024], F32)
        p.psB = psf("psB", [128, 1024], F32)
        p.psC = psf("psC", [128, 1024], F32)
        p.psD = psf("psD", [128, 512], F32)

        p.s_ld = Ctr(es.enter_context(nc.semaphore("s_ld")), 16)
        p.s_gd = Ctr(es.enter_context(nc.semaphore("s_gd")), 16)
        p.s_t = Ctr(es.enter_context(nc.semaphore("s_t")), 1)
        p.s_v = Ctr(es.enter_context(nc.semaphore("s_v")), 1)
        p.s_s = Ctr(es.enter_context(nc.semaphore("s_s")), 1)
        p.s_g = Ctr(es.enter_context(nc.semaphore("s_g")), 1)
        p.s_cc = Ctr(es.enter_context(nc.semaphore("s_cc")), 1)
        p.s_st = Ctr(es.enter_context(nc.semaphore("s_st")), 16)

        nc.const_aps.register(F32, 1e-5, p.small[:, 4:5])
        p.block = es.enter_context(nc.Block())
        emit_program(p)

    return nc


def emit_program(p):
    io, ld, st = p.io, p.s_ld, p.s_st

    def loads(s):
        for ch in range(NCH):
            ld.bump(s.dma_start(out=p.xs[:, ch * TOK:(ch + 1) * TOK],
                                in_=io["x_own"][128 * ch:128 * (ch + 1), :]))
        ld.bump(s.dma_start(out=p.rolei[:, :], in_=io["role"][:, :]))
        ld.bump(s.dma_start(out=p.biases[:, 0:24], in_=io["bqkv"][:, :]))
        ld.bump(s.dma_start(out=p.biases[:, 24:32], in_=io["bout"][:, :]))
        ld.bump(s.dma_start(out=p.biases[:, 32:64], in_=io["b_fc"][:, :]))
        ld.bump(s.dma_start(out=p.biases[:, 64:72], in_=io["b_proj"][:, :]))
        ld.bump(s.dma_start(out=p.biases[:, 72:104], in_=io["W1b"][:, :]))
        ld.bump(s.dma_start(out=p.biases[:, 104:112], in_=io["W2b"][:, :]))
    p.block.sync(loads)

    def g_vb(g):
        p.s_gd.bump(g.dma_start(out=p.vbias[0:1, :], in_=io["bqkv_row"][0:1, :]))
        g.wait_ge(p.s_gd.sem, p.s_gd.v)
    p.block.gpsimd(g_vb)
    p.t_vb = p.s_gd.tok()
    p.t_base = ld.tok()

    if STAGE == 0:
        def store0(s):
            waitall(s, [p.t_base])
            for ch in range(NCH):
                st.bump(s.dma_start(out=io["x_out"][128 * ch:128 * (ch + 1), :],
                                    in_=p.xs[:, ch * TOK:(ch + 1) * TOK]))
            s.wait_ge(st.sem, st.v)
        p.block.sync(store0)
        return

    with p.nc.named_scope("consts"):
        emit_constants(p)
    with p.nc.named_scope("l0attn"):
        emit_l0_attn(p)
    if STAGE >= 2:
        with p.nc.named_scope("quant"):
            emit_quant(p)
        if STAGE >= 3:
            emit_l1_weights_early(p)
        with p.nc.named_scope("l0ffn"):
            emit_ffn(p, layer=0)
    if STAGE >= 3:
        with p.nc.named_scope("l1attn"):
            emit_l1_attn(p)
    if STAGE >= 4:
        with p.nc.named_scope("l1ffn"):
            emit_ffn(p, layer=1)

    sv, ss = p.s_v, p.s_s

    def storef(s):
        s.wait_ge(sv.sem, sv.v)
        s.wait_ge(ss.sem, ss.v)
        for ch in range(NCH):
            st.bump(s.dma_start(out=io["x_out"][128 * ch:128 * (ch + 1), :],
                                in_=p.xs[:, ch * TOK:(ch + 1) * TOK]))
        s.wait_ge(st.sem, st.v)
    p.block.sync(storef)


def emit_constants(p):
    gr, sv = p.s_g, p.s_v

    def g_consts(g):
        c = p.consts
        g.memset(c[:, 0:5376], 1.0)
        g.drain()
        g.affine_select(out=c[:, 0:128], in_=c[:, 0:128], pattern=[[-1, 128]],
                        compare_op=ALU.is_equal, fill=0.0, base=0, channel_multiplier=1)
        for j in range(8):
            m = c[:, 256 + 512 * j:256 + 512 * (j + 1)]
            g.affine_select(out=m, in_=m, pattern=[[-1, 512]],
                            compare_op=ALU.is_ge, fill=0.0, base=128 * j, channel_multiplier=1)
        g.drain()
        for j in range(8):
            m = c[:, 256 + 512 * j:256 + 512 * (j + 1)]
            g.affine_select(out=m, in_=m, pattern=[[1, 512]],
                            compare_op=ALU.is_ge, fill=0.0, base=511 - 128 * j, channel_multiplier=-1)
        g.affine_select(out=c[:, 4352:5376], in_=c[:, 4352:5376],
                        pattern=[[-1, 16], [0, 64]], compare_op=ALU.is_equal,
                        fill=0.0, base=0, channel_multiplier=1)
        gr.bump(g.drain())
    p.block.gpsimd(g_consts)
    p.t_consts = gr.tok()

    def v_init(v):
        waitall(v, [p.t_base])
        v.memset(p.small[:, 0:2], 1.0)
        v.memset(p.small[:, 4:5], 1e-5)
        v.memset(p.onesr[:, :], 1.0)
        v.drain()
        v.tensor_scalar(out=p.small[:, 8:16], in0=p.biases[:, 0:8],
                        scalar1=0.125, scalar2=None, op0=ALU.mult)
        sv.bump(v.drain())
    p.block.vector(v_init)
    p.t_vinit = sv.tok()

    p.C_ID = 0
    p.C_ONES = 128
    p.C_L0M = 256
    p.C_E16 = 4352
    p.C_SCR = 5376



def emit_ln(p, segs, dst_col, pre_toks=(), unit=512):
    stn, sv, ss = p.s_t, p.s_v, p.s_s
    LN_XB, LN_SQ = p.LN_XB, p.LN_SQ
    total = sum(w for (_, _, w) in segs)
    assert total % unit == 0
    nunit = total // unit
    units = []
    for u in range(nunit):
        lo, hi = u * unit, (u + 1) * unit
        cur, pos = [], 0
        for (kind, col0, w) in segs:
            s0, s1 = max(lo, pos), min(hi, pos + w)
            if s1 > s0:
                cur.append((kind, col0 + (s0 - pos), s1 - s0, s0 - lo))
            pos += w
        units.append(cur)

    def src_ap(kind, col0, w, ch):
        if kind == "x":
            return p.xs[:, col0 + ch * TOK: col0 + ch * TOK + w]
        return p.aarena[:, col0 + ch * 512: col0 + ch * 512 + w]

    for u, segl in enumerate(units):
        def v_cast(v, segl=segl, u=u):
            if u == 0:
                waitall(v, list(pre_toks))
            for ch in range(NCH):
                for (kind, col0, w, off) in segl:
                    v.tensor_copy(out=p.aarena[:, LN_XB + ch * 512 + off: LN_XB + ch * 512 + off + w],
                                  in_=src_ap(kind, col0, w, ch))
            sv.bump(v.drain())
        p.block.vector(v_cast)
        t_cast = sv.tok()

        def s_sq(s):
            waitall(s, [t_cast])
            for ch in range(NCH):
                s.activation(out=p.aarena[:, LN_SQ + ch * 512: LN_SQ + ch * 512 + unit],
                             in_=p.aarena[:, LN_XB + ch * 512: LN_XB + ch * 512 + unit],
                             func=AF.Square)
            ss.bump(s.drain())
        p.block.scalar(s_sq)
        t_sq = ss.tok()

        def t_sums(t):
            waitall(t, [t_sq, p.t_consts])
            for ch in range(NCH):
                t.matmul(p.psA[0:1, 0:unit], p.consts[:, p.C_ONES:p.C_ONES + 1],
                         p.aarena[:, LN_XB + ch * 512: LN_XB + ch * 512 + unit],
                         start=(ch == 0), stop=(ch == NCH - 1))
            for ch in range(NCH):
                mm = t.matmul(p.psA[0:1, 512:512 + unit], p.consts[:, p.C_ONES:p.C_ONES + 1],
                              p.aarena[:, LN_SQ + ch * 512: LN_SQ + ch * 512 + unit],
                              start=(ch == 0), stop=(ch == NCH - 1))
            stn.bump(mm)
        p.block.tensor(t_sums)
        t_sum = stn.tok()

        def v_stats(v):
            waitall(v, [t_sum])
            v.tensor_scalar(out=p.fsc[0:1, 0:unit], in0=p.psA[0:1, 0:unit],
                            scalar1=1.0 / D, scalar2=None, op0=ALU.mult)
            v.tensor_scalar(out=p.fsc[0:1, 512:512 + unit], in0=p.psA[0:1, 512:512 + unit],
                            scalar1=1.0 / D, scalar2=None, op0=ALU.mult)
            v.drain()
            v.tensor_tensor(out=p.psA[0:1, 0:unit], in0=p.fsc[0:1, 0:unit],
                            in1=p.fsc[0:1, 0:unit], op=ALU.mult)
            v.drain()
            v.tensor_tensor(out=p.fsc[0:1, 512:512 + unit], in0=p.fsc[0:1, 512:512 + unit],
                            in1=p.psA[0:1, 0:unit], op=ALU.subtract)
            sv.bump(v.drain())
        p.block.vector(v_stats)
        t_var = sv.tok()

        def s_sqrt(s):
            waitall(s, [t_var])
            s.activation(out=p.fsc[0:1, 512:512 + unit], in_=p.fsc[0:1, 512:512 + unit],
                         func=AF.Sqrt, bias=1e-5, scale=1.0)
            ss.bump(s.drain())
        p.block.scalar(s_sqrt)
        t_sqrt = ss.tok()

        def v_recip(v):
            waitall(v, [t_sqrt])
            v.reciprocal(out=p.fsc[0:1, 512:512 + unit], in_=p.fsc[0:1, 512:512 + unit])
            sv.bump(v.drain())
        p.block.vector(v_recip)
        t_rstd = sv.tok()

        def t_bcast(t):
            waitall(t, [t_rstd])
            t.matmul(p.psB[:, 0:unit], p.onesr[0:1, 0:128], p.fsc[0:1, 0:unit],
                     start=True, stop=True)
            stn.bump(t.matmul(p.psB[:, 512:512 + unit], p.onesr[0:1, 0:128],
                              p.fsc[0:1, 512:512 + unit], start=True, stop=True))
        p.block.tensor(t_bcast)
        t_bc = stn.tok()

        def v_apply(v, segl=segl, u=u):
            waitall(v, [t_bc])
            for ch in range(NCH):
                for (kind, col0, w, off) in segl:
                    v.tensor_tensor(out=p.fsc[:, 0:w], in0=src_ap(kind, col0, w, ch),
                                    in1=p.psB[:, off:off + w], op=ALU.subtract)
                    v.drain()
                    dc = dst_col + ch * total + u * unit + off
                    v.tensor_tensor(out=p.aarena[:, dc:dc + w], in0=p.fsc[:, 0:w],
                                    in1=p.psB[:, 512 + off:512 + off + w], op=ALU.mult)
                    v.drain()
            sv.bump(v.drain())
        p.block.vector(v_apply)
    return sv.tok()



def emit_l0_attn(p):
    io = p.io
    ld, gd = p.s_ld, p.s_gd
    stn, sv, ss = p.s_t, p.s_v, p.s_s

    XHW = 0
    QB = 8192
    KB = 12288
    VP = 20480
    PT = 28800
    LB = 29824
    p.LN_XB = 8192
    p.LN_SQ = 12288
    HALO = 16384

    def g_w(g):
        for part in range(4):
            gd.bump(g.dma_start(out=p.warena[:, part * 6144:(part + 1) * 6144],
                                in_=io["Wqkv_t"][:, part * 6144:(part + 1) * 6144]))
        for part in range(2):
            gd.bump(g.dma_start(out=p.warena[:, 24576 + part * 4096:24576 + (part + 1) * 4096],
                                in_=io["Wout_t"][:, part * 4096:(part + 1) * 4096]))
        g.wait_ge(gd.sem, gd.v)
    p.block.gpsimd(g_w)
    t_w = gd.tok()

    def wtile(oc, ch):
        return p.warena[:, oc * 1024 + ch * 128: oc * 1024 + (ch + 1) * 128]

    def wotile(oc, ch):
        return p.warena[:, 24576 + oc * 1024 + ch * 128: 24576 + oc * 1024 + (ch + 1) * 128]

    for bl in range(2):
        for cq in range(2):
            ck = 2 * bl + cq
            own0 = bl * 1024 + cq * 512
            def g_halo(g, bl=bl, cq=cq):
                g.wait_ge(sv.sem, sv.v)
                g.wait_ge(ss.sem, ss.v)
                for ch in range(NCH):
                    if cq == 0:
                        gd.bump(g.dma_start(
                            out=p.aarena[:, HALO + ch * 512: HALO + ch * 512 + 256],
                            in_=io["x_halo"][128 * ch:128 * (ch + 1), bl * 512: bl * 512 + 256]))
                    else:
                        gd.bump(g.dma_start(
                            out=p.aarena[:, HALO + ch * 512: HALO + ch * 512 + 256],
                            in_=io["x_own"][128 * ch:128 * (ch + 1),
                                            bl * 1024 + 256: bl * 1024 + 512]))
                        gd.bump(g.dma_start(
                            out=p.aarena[:, HALO + ch * 512 + 256: HALO + ch * 512 + 512],
                            in_=io["x_halo"][128 * ch:128 * (ch + 1),
                                             bl * 512 + 256: bl * 512 + 512]))
                g.wait_ge(gd.sem, gd.v)
            p.block.gpsimd(g_halo)
            t_halo = gd.tok()

            if cq == 0:
                segs = [("a", HALO, 256), ("x", bl * 1024, 768)]
            else:
                segs = [("a", HALO, 256), ("x", bl * 1024 + 512, 512), ("a", HALO + 256, 256)]
            t_ln = emit_ln(p, segs, XHW, pre_toks=(p.t_base, t_halo, p.t_vinit, t_w))

            for j in range(8):
                half = j % 2

                def t_q(t, j=j, half=half):
                    if j == 0:
                        waitall(t, [t_ln])
                    t.wait_ge(ss.sem, ss.v)
                    for ch in range(NCH):
                        mm = t.matmul(p.psB[:, half * 512:half * 512 + 512], wtile(j, ch),
                                      p.aarena[:, XHW + ch * 1024 + 256: XHW + ch * 1024 + 768],
                                      start=(ch == 0), stop=(ch == NCH - 1))
                    stn.bump(mm)
                p.block.tensor(t_q)
                t_mm = stn.tok()

                def s_qe(s, j=j, half=half):
                    waitall(s, [t_mm])
                    s.activation(out=p.aarena[:, QB + j * 512: QB + (j + 1) * 512],
                                 in_=p.psB[:, half * 512:half * 512 + 512],
                                 func=AF.Identity, bias=p.small[:, 8 + j:9 + j], scale=0.125)
                    ss.bump(s.drain())
                p.block.scalar(s_qe)

            for j in range(8):
                for un in range(2):
                    half = (2 * j + un) % 2

                    def t_k(t, j=j, un=un, half=half):
                        t.wait_ge(ss.sem, ss.v)
                        for ch in range(NCH):
                            mm = t.matmul(p.psB[:, half * 512:half * 512 + 512], wtile(8 + j, ch),
                                          p.aarena[:, XHW + ch * 1024 + un * 512: XHW + ch * 1024 + (un + 1) * 512],
                                          start=(ch == 0), stop=(ch == NCH - 1))
                        stn.bump(mm)
                    p.block.tensor(t_k)
                    t_mm = stn.tok()

                    def s_ke(s, j=j, un=un, half=half):
                        waitall(s, [t_mm])
                        s.activation(out=p.aarena[:, KB + j * 1024 + un * 512: KB + j * 1024 + (un + 1) * 512],
                                     in_=p.psB[:, half * 512:half * 512 + 512],
                                     func=AF.Identity, bias=p.biases[:, 8 + j:9 + j], scale=1.0)
                        ss.bump(s.drain())
                    p.block.scalar(s_ke)

            for kt in range(8):
                for vh in range(2):
                    half = (2 * kt + vh) % 2

                    def t_v(t, kt=kt, vh=vh, half=half):
                        t.wait_ge(sv.sem, sv.v)
                        t.matmul(p.psA[:, half * 512:half * 512 + 512],
                                 p.consts[0:1, p.C_ONES:p.C_ONES + 128],
                                 p.vbias[0:1, vh * 512:(vh + 1) * 512], start=True, stop=False)
                        for ch in range(NCH):
                            base = (16 + 4 * vh) * 1024 + ch * 128
                            rhs = p.warena[:, base:base + 4096] \
                                .rearrange("p (o c) -> p o c", o=4)[:, :, 0:128]
                            mm = t.matmul(p.psA[:, half * 512:half * 512 + 512],
                                          p.aarena[:, XHW + ch * 1024 + kt * 128: XHW + ch * 1024 + (kt + 1) * 128],
                                          rhs, start=False, stop=(ch == NCH - 1))
                        stn.bump(mm)
                    p.block.tensor(t_v)
                    t_mm = stn.tok()

                    def v_ve(v, kt=kt, vh=vh, half=half):
                        waitall(v, [t_mm])
                        dst = p.aarena[:, VP + kt * 1040 + vh * 8 * 65: VP + kt * 1040 + (vh + 1) * 8 * 65]
                        v.tensor_copy(out=dst.rearrange("p (h c) -> p h c", h=8)[:, :, 0:64],
                                      in_=p.psA[:, half * 512:half * 512 + 512].rearrange("p (h c) -> p h c", h=8))
                        sv.bump(v.drain())
                    p.block.vector(v_ve)

            def v_vones(v, ck=ck):
                v.wait_ge(sv.sem, sv.v)
                v.wait_ge(ss.sem, ss.v)
                on = p.aarena[:, VP:VP + 8 * 1040]
                v.memset(on.rearrange("p (k h c) -> p (k h) c", k=8, h=16)[:, :, 64:65], 1.0)
                v.drain()
                with v.register(f"role{ck}") as r:
                    v.reg_load(r, p.rolei[0:1, 0:1])
                    with v.If_eq(r, 0):
                        if ck == 0:
                            for j in range(8):
                                v.memset(p.aarena[:, KB + j * 1024:KB + j * 1024 + 256], 0.0)
                            for kt in range(2):
                                vv = p.aarena[:, VP + kt * 1040: VP + (kt + 1) * 1040]
                                v.memset(vv.rearrange("p (h c) -> p h c", h=16)[:, :, 0:64], 0.0)
                        elif ck == 3:
                            for j in range(8):
                                v.memset(p.aarena[:, KB + j * 1024 + 768:KB + (j + 1) * 1024], 0.0)
                            for kt in range(6, 8):
                                vv = p.aarena[:, VP + kt * 1040: VP + (kt + 1) * 1040]
                                v.memset(vv.rearrange("p (h c) -> p h c", h=16)[:, :, 0:64], 0.0)
                        else:
                            v.nop()
                    with v.Else():
                        v.nop()
                    v.end_ifs()
                sv.bump(v.drain())
            p.block.vector(v_vones)
            t_kv = sv.tok()

            red_toks = []
            for hp in range(8):
                for kt in range(8):
                    pa = p.psA if kt % 2 == 0 else p.psB

                    def t_pa(t, hp=hp, kt=kt, pa=pa):
                        if hp == 0 and kt == 0:
                            waitall(t, [t_kv])
                            t.wait_ge(ss.sem, ss.v)
                        idx = hp * 8 + kt
                        if idx >= 2:
                            waitall(t, [red_toks[idx - 2]])
                        t.matmul(pa[:, 0:512],
                                 p.aarena[0:64, KB + hp * 1024 + kt * 128: KB + hp * 1024 + (kt + 1) * 128],
                                 p.aarena[0:64, QB + hp * 512: QB + (hp + 1) * 512],
                                 start=True, stop=True, tile_position=(0, 0))
                        stn.bump(t.matmul(pa[:, 512:1024],
                                          p.aarena[64:128, KB + hp * 1024 + kt * 128: KB + hp * 1024 + (kt + 1) * 128],
                                          p.aarena[64:128, QB + hp * 512: QB + (hp + 1) * 512],
                                          start=True, stop=True, tile_position=(64, 0)))
                    p.block.tensor(t_pa)
                    t_pamm = stn.tok()

                    def v_red(v, hp=hp, kt=kt, pa=pa):
                        waitall(v, [t_pamm])
                        v.reduce_max(out=p.small[:, 16 + hp * 8 + kt:17 + hp * 8 + kt],
                                     in_=pa[:, 0:1024], axis=AX.X)
                        sv.bump(v.drain())
                    p.block.vector(v_red)
                    red_toks.append(sv.tok())

            def g_pmax(g):
                g.wait_ge(sv.sem, sv.v)
                for hp in range(8):
                    g.reduce_max(out=p.small[0:1, 89 + hp:90 + hp],
                                 in_=p.small[:, 16 + 8 * hp:24 + 8 * hp],
                                 axis=AX.XYZWC)
                p.s_g.bump(g.drain())
            p.block.gpsimd(g_pmax)
            t_mrow = p.s_g.tok()

            def t_mxb(t):
                waitall(t, [t_mrow])
                stn.bump(t.matmul(p.psD[:, 0:8], p.onesr[0:1, 0:128], p.small[0:1, 89:97],
                                  start=True, stop=True))
            p.block.tensor(t_mxb)
            t_mxb_t = stn.tok()

            def s_mxe(s):
                waitall(s, [t_mxb_t])
                s.activation(out=p.small[:, 96:104], in_=p.psD[:, 0:8],
                             func=AF.Identity, bias=0.0, scale=-1.0)
                ss.bump(s.drain())
            p.block.scalar(s_mxe)
            t_bias = ss.tok()

            for hp in range(8):
                mm_base = stn.v
                exp_base = ss.v
                msk_base = sv.v
                sv_prev = sv.v

                def t_pbh(t, hp=hp, mm_base=mm_base, msk_base=msk_base,
                          sv_prev=sv_prev):
                    if hp == 0:
                        waitall(t, [t_bias])
                    for kt in range(8):
                        if kt == 0:
                            t.wait_ge(sv.sem, sv_prev)
                        else:
                            t.wait_ge(sv.sem, msk_base + kt)
                        t.matmul(p.psA[:, 0:512],
                                 p.aarena[0:64, KB + hp * 1024 + kt * 128: KB + hp * 1024 + (kt + 1) * 128],
                                 p.aarena[0:64, QB + hp * 512: QB + (hp + 1) * 512],
                                 start=True, stop=True, tile_position=(0, 0))
                        t.matmul(p.psA[:, 512:1024],
                                 p.aarena[64:128, KB + hp * 1024 + kt * 128: KB + hp * 1024 + (kt + 1) * 128],
                                 p.aarena[64:128, QB + hp * 512: QB + (hp + 1) * 512],
                                 start=True, stop=True, tile_position=(64, 0)
                                 ).then_inc(stn.sem, 1)
                        t.wait_ge(sv.sem, msk_base + kt + 1)
                        t.matmul(p.psC[0:65, 0:512],
                                 p.aarena[:, VP + kt * 1040 + (2 * hp) * 65: VP + kt * 1040 + (2 * hp + 1) * 65],
                                 p.aarena[:, PT:PT + 512],
                                 start=(kt == 0), stop=(kt == 7))
                        t.matmul(p.psC[0:65, 512:1024],
                                 p.aarena[:, VP + kt * 1040 + (2 * hp + 1) * 65: VP + kt * 1040 + (2 * hp + 2) * 65],
                                 p.aarena[:, PT + 512:PT + 1024],
                                 start=(kt == 0), stop=(kt == 7)
                                 ).then_inc(stn.sem, 1)
                p.block.tensor(t_pbh)
                stn.v += 16
                t_avd = stn.tok()

                def s_exph(s, hp=hp, mm_base=mm_base):
                    for kt in range(8):
                        s.wait_ge(stn.sem, mm_base + 2 * kt + 1)
                        s.activation(out=p.aarena[:, PT:PT + 1024], in_=p.psA[:, 0:1024],
                                     func=AF.Exp, bias=p.small[:, 96 + hp:97 + hp], scale=1.0)
                        s.drain().then_inc(ss.sem, 1)
                p.block.scalar(s_exph)
                ss.v += 8

                def v_mskh(v, hp=hp, exp_base=exp_base):
                    for kt in range(8):
                        v.wait_ge(ss.sem, exp_base + kt + 1)
                        m = p.consts[:, p.C_L0M + kt * 512: p.C_L0M + (kt + 1) * 512]
                        v.tensor_tensor(out=p.aarena[:, PT:PT + 512], in0=p.aarena[:, PT:PT + 512],
                                        in1=m, op=ALU.mult)
                        v.tensor_tensor(out=p.aarena[:, PT + 512:PT + 1024],
                                        in0=p.aarena[:, PT + 512:PT + 1024], in1=m, op=ALU.mult)
                        v.drain().then_inc(sv.sem, 1)
                p.block.vector(v_mskh)
                sv.v += 8

                def v_lrec(v, hp=hp):
                    waitall(v, [t_avd])
                    v.tensor_copy(out=p.fsc[64:65, 0:1024], in_=p.psC[64:65, 0:1024])
                    v.drain()
                    v.reciprocal(out=p.fsc[64:65, 0:1024], in_=p.fsc[64:65, 0:1024])
                    sv.bump(v.drain())
                p.block.vector(v_lrec)
                t_rb = sv.tok()

                def t_rbc(t, hp=hp):
                    waitall(t, [t_rb])
                    t.matmul(p.psD[0:64, 0:512], p.onesr[64:65, 0:64],
                             p.fsc[64:65, 0:512], start=True, stop=True)
                    stn.bump(t.matmul(p.psB[0:64, 0:512], p.onesr[64:65, 0:64],
                                      p.fsc[64:65, 512:1024], start=True, stop=True))
                p.block.tensor(t_rbc)
                t_rp = stn.tok()

                def v_y(v, hp=hp):
                    waitall(v, [t_rp])
                    v.tensor_copy(out=p.aarena[0:64, PT:PT + 512], in_=p.psD[0:64, 0:512])
                    v.tensor_copy(out=p.aarena[0:64, PT + 512:PT + 1024], in_=p.psB[0:64, 0:512])
                    v.drain()
                    v.tensor_tensor(out=p.aarena[0:64, XHW + (2 * hp) * 512: XHW + (2 * hp) * 512 + 512],
                                    in0=p.psC[0:64, 0:512], in1=p.aarena[0:64, PT:PT + 512], op=ALU.mult)
                    v.tensor_tensor(out=p.aarena[0:64, XHW + (2 * hp + 1) * 512: XHW + (2 * hp + 1) * 512 + 512],
                                    in0=p.psC[0:64, 512:1024], in1=p.aarena[0:64, PT + 512:PT + 1024], op=ALU.mult)
                    sv.bump(v.drain())
                p.block.vector(v_y)
            t_yall = sv.tok()

            def g_yasm(g):
                waitall(g, [t_yall])
                src = p.aarena[0:64, XHW:XHW + 8192].rearrange("p (h two c) -> p h two c", two=2, c=512)
                dst_e = p.aarena[0:64, QB:QB + 4096].rearrange("p (h c) -> p h c", c=512)
                gd.bump(g.dma_start(out=dst_e, in_=src[:, :, 0:1, :].rearrange("p h two c -> p (h two) c")))
                dst_o = p.aarena[64:128, QB:QB + 4096].rearrange("p (h c) -> p h c", c=512)
                gd.bump(g.dma_start(out=dst_o, in_=src[:, :, 1:2, :].rearrange("p h two c -> p (h two) c")))
                g.wait_ge(gd.sem, gd.v)
            p.block.gpsimd(g_yasm)
            t_yq = gd.tok()

            for oc in range(8):
                half = oc % 2

                def t_o(t, oc=oc, half=half):
                    if oc == 0:
                        waitall(t, [t_yq])
                    if oc >= 2:
                        t.wait_ge(sv.sem, sv.v)
                    for ch in range(NCH):
                        mm = t.matmul(p.psB[:, half * 512:half * 512 + 512], wotile(oc, ch),
                                      p.aarena[:, QB + ch * 512: QB + (ch + 1) * 512],
                                      start=(ch == 0), stop=(ch == NCH - 1))
                    stn.bump(mm)
                p.block.tensor(t_o)
                t_om = stn.tok()

                def v_res(v, oc=oc, half=half, own0=own0):
                    waitall(v, [t_om])
                    v.scalar_tensor_tensor(
                        out=p.xs[:, oc * TOK + own0: oc * TOK + own0 + 512],
                        in0=p.psB[:, half * 512:half * 512 + 512],
                        scalar=p.biases[:, 24 + oc:25 + oc],
                        in1=p.xs[:, oc * TOK + own0: oc * TOK + own0 + 512],
                        op0=ALU.add, op1=ALU.add)
                    sv.bump(v.drain())
                p.block.vector(v_res)


def emit_l0_ffn(p):
    emit_quant(p)
    emit_ffn(p, layer=0)


def emit_quant(p):
    io, ld, gd = p.io, p.s_ld, p.s_gd
    stn, sv, ss, scc = p.s_t, p.s_v, p.s_s, p.s_cc
    WQ = {"fc": 0, "proj": 4096, "w1": 8192, "w2": 12288}
    SRC = {"fc": "WfcTs", "proj": "WprojTs", "w1": "W1Ts", "w2": "W2Ts"}

    for wi, wn in enumerate(["fc", "proj", "w1", "w2"]):
        for sub in range(8):
            def s_load(s, wn=wn, sub=sub):
                s.wait_ge(sv.sem, sv.v)
                ld.bump(s.dma_start(out=p.wsl[:, 0:512],
                                    in_=io[SRC[wn]][:, 512 * sub:512 * (sub + 1)]))
            p.block.sync(s_load)
            t_l = ld.tok()

            def v_red(v, wn=wn, sub=sub, t_l=t_l):
                waitall(v, [t_l])
                if wn in ("fc", "proj"):
                    v.reduce_max(out=p.small[:, 48 + sub:49 + sub], in_=p.wsl[:, 0:512],
                                 axis=AX.X, apply_absolute_value=True)
                else:
                    v.reduce_sum(out=p.small[:, 48 + sub:49 + sub], in_=p.wsl[:, 0:512],
                                 axis=AX.X, apply_absolute_value=True)
                sv.bump(v.drain())
            p.block.vector(v_red)

        def v_comb(v, wi=wi, wn=wn):
            v.wait_ge(sv.sem, sv.v)
            if wn in ("fc", "proj"):
                v.reduce_max(out=p.small[:, 56 + wi:57 + wi], in_=p.small[:, 48:56], axis=AX.X)
            else:
                v.reduce_sum(out=p.small[:, 56 + wi:57 + wi], in_=p.small[:, 48:56], axis=AX.X)
            sv.bump(v.drain())
        p.block.vector(v_comb)
    t_part = sv.tok()

    def g_gather(g):
        waitall(g, [t_part])
        for wi in range(2):
            gd.bump(g.dma_start(out=p.fsc[0:1, 128 * wi:128 * (wi + 1)],
                                in_=p.small[:, 56 + wi:57 + wi]))
        g.wait_ge(gd.sem, gd.v)
    p.block.gpsimd(g_gather)
    t_gath = gd.tok()

    def v_cross(v):
        waitall(v, [t_gath])
        v.reduce_max(out=p.small[0:1, 20:21], in_=p.fsc[0:1, 0:128], axis=AX.X)
        v.reduce_max(out=p.small[0:1, 21:22], in_=p.fsc[0:1, 128:256], axis=AX.X)
        sv.bump(v.drain())
    p.block.vector(v_cross)
    t_vx = sv.tok()

    def t_cross(t):
        waitall(t, [t_part])
        t.matmul(p.psD[0:1, 0:1], p.onesr[:, 0:1], p.small[:, 58:59], start=True, stop=True)
        stn.bump(t.matmul(p.psD[0:1, 1:2], p.onesr[:, 0:1], p.small[:, 59:60],
                          start=True, stop=True))
    p.block.tensor(t_cross)
    t_tx = stn.tok()

    def s_sums(s):
        waitall(s, [t_tx])
        s.activation(out=p.small[0:1, 22:24], in_=p.psD[0:1, 0:2], func=AF.Identity,
                     bias=0.0, scale=1.0)
        ss.bump(s.drain())
    p.block.scalar(s_sums)
    t_sx = ss.tok()

    def g_ag1(g):
        waitall(g, [t_vx, t_sx])
        gd.bump(g.dma_start(out=p.cc_qin[0:4], in_=p.small[0:1, 20:24]))
        g.wait_ge(gd.sem, gd.v)
        scc.bump(g.collective_compute(
            "AllGather", ALU.bypass, replica_groups=[list(range(NCORES))],
            ins=[p.cc_qin[:]], outs=[p.cc_qout[:]]))
        g.wait_ge(scc.sem, scc.v)
        gd.bump(g.dma_start(out=p.fsc[0:1, 0:32], in_=p.cc_qout[:]))
        g.wait_ge(gd.sem, gd.v)
    p.block.gpsimd(g_ag1)
    t_ag1 = gd.tok()

    def v_scales(v):
        waitall(v, [t_ag1])
        r8 = p.fsc[0:1, 0:32].rearrange("p (r k) -> p k r", r=8)
        v.reduce_max(out=p.small[0:1, 24:25], in_=r8[:, 0:1, :], axis=AX.X)
        v.reduce_max(out=p.small[0:1, 25:26], in_=r8[:, 1:2, :], axis=AX.X)
        v.reduce_sum(out=p.small[0:1, 26:27], in_=r8[:, 2:3, :], axis=AX.X)
        v.reduce_sum(out=p.small[0:1, 27:28], in_=r8[:, 3:4, :], axis=AX.X)
        v.drain()
        v.tensor_scalar(out=p.small[0:1, 24:26], in0=p.small[0:1, 24:26],
                        scalar1=1.0 / 127.0, scalar2=None, op0=ALU.mult)
        v.tensor_scalar(out=p.small[0:1, 26:28], in0=p.small[0:1, 26:28],
                        scalar1=1.0 / (4096.0 * 1024.0), scalar2=1e-8,
                        op0=ALU.mult, op1=ALU.max)
        v.drain()
        v.reciprocal(out=p.small[0:1, 28:32], in_=p.small[0:1, 24:28])
        sv.bump(v.drain())
    p.block.vector(v_scales)
    t_sc = sv.tok()

    def t_bc(t):
        waitall(t, [t_sc])
        stn.bump(t.matmul(p.psD[:, 0:8], p.onesr[0:1, 0:128], p.small[0:1, 24:32],
                          start=True, stop=True))
    p.block.tensor(t_bc)
    t_bc_t = stn.tok()

    def s_bc(s):
        waitall(s, [t_bc_t])
        s.activation(out=p.small[:, 32:40], in_=p.psD[:, 0:8], func=AF.Identity,
                     bias=0.0, scale=1.0)
        ss.bump(s.drain())
    p.block.scalar(s_bc)
    t_bcc = ss.tok()

    def v_2rs(v):
        waitall(v, [t_bcc])
        v.tensor_scalar(out=p.small[:, 40:42], in0=p.small[:, 36:38],
                        scalar1=2.0, scalar2=None, op0=ALU.mult)
        sv.bump(v.drain())
    p.block.vector(v_2rs)
    t_2rs = sv.tok()

    for wi, wn in enumerate(["fc", "proj", "w1", "w2"]):
        for sub in range(8):
            def s_load2(s, wn=wn, sub=sub):
                s.wait_ge(sv.sem, sv.v)
                ld.bump(s.dma_start(out=p.wsl[:, 0:512],
                                    in_=io[SRC[wn]][:, 512 * sub:512 * (sub + 1)]))
            p.block.sync(s_load2)
            t_l = ld.tok()
            dst = p.aarena[:, WQ[wn] + 512 * sub: WQ[wn] + 512 * (sub + 1)]
            rs_col = p.small[:, 36 + wi:37 + wi]

            if wn in ("fc", "proj"):
                def v_fq(v, rs_col=rs_col, dst=dst, t_l=t_l):
                    waitall(v, [t_l, t_2rs])
                    z = p.fsc[:, 0:512]
                    v.tensor_scalar(out=z, in0=p.wsl[:, 0:512], scalar1=rs_col,
                                    scalar2=128.5, op0=ALU.mult, op1=ALU.add)
                    v.drain()
                    v.tensor_scalar(out=z, in0=z, scalar1=255.99, scalar2=0.5,
                                    op0=ALU.min, op1=ALU.max)
                    v.drain()
                    v.tensor_copy(out=p.fsc[:, 512:1024], in_=z)
                    v.drain()
                    for bit in (128.0, 64.0, 32.0, 16.0, 8.0, 4.0, 2.0, 1.0):
                        v.tensor_scalar(out=p.psD[:, 0:512], in0=z, scalar1=bit,
                                        scalar2=bit, op0=ALU.is_ge, op1=ALU.mult)
                        v.drain()
                        v.tensor_tensor(out=z, in0=z, in1=p.psD[:, 0:512], op=ALU.subtract)
                        v.drain()
                    v.tensor_tensor(out=p.fsc[:, 512:1024], in0=p.fsc[:, 512:1024],
                                    in1=z, op=ALU.subtract)
                    v.drain()
                    v.tensor_scalar(out=dst, in0=p.fsc[:, 512:1024], scalar1=-128.0,
                                    scalar2=None, op0=ALU.add)
                    sv.bump(v.drain())
                p.block.vector(v_fq)
            else:
                def v_tn(v, rs_col=rs_col, dst=dst, t_l=t_l):
                    waitall(v, [t_l, t_2rs])
                    v.tensor_scalar(out=p.fsc[:, 0:512], in0=p.wsl[:, 0:512],
                                    scalar1=rs_col, scalar2=0.5, op0=ALU.mult, op1=ALU.is_gt)
                    v.tensor_scalar(out=p.fsc[:, 512:1024], in0=p.wsl[:, 0:512],
                                    scalar1=rs_col, scalar2=-0.5, op0=ALU.mult, op1=ALU.is_lt)
                    v.drain()
                    v.tensor_tensor(out=dst, in0=p.fsc[:, 0:512], in1=p.fsc[:, 512:1024],
                                    op=ALU.subtract)
                    sv.bump(v.drain())
                p.block.vector(v_tn)
    t_q = sv.tok()

    def g_ship(g):
        waitall(g, [t_q])
        for wi, wn in enumerate(["fc", "proj", "w1", "w2"]):
            gd.bump(g.dma_start(
                out=p.cc_win[wi * 524288:(wi + 1) * 524288].rearrange("(q c) -> q c", q=128),
                in_=p.aarena[:, WQ[wn]:WQ[wn] + 4096]))
        g.wait_ge(gd.sem, gd.v)
        scc.bump(g.collective_compute(
            "AllGather", ALU.bypass, replica_groups=[list(range(NCORES))],
            ins=[p.cc_win[:]], outs=[p.cc_wout[:]]))
        g.wait_ge(scc.sem, scc.v)
    p.block.gpsimd(g_ship)
    p.t_qship = p.s_gd.tok()
    p.t_wq = p.s_cc.tok()


def wq_dram(p, wn, rank):
    wi = {"fc": 0, "proj": 1, "w1": 2, "w2": 3}[wn]
    base = rank * 4 * 524288 + wi * 524288
    return p.cc_wout[base:base + 524288].rearrange("(q c) -> q c", q=128)


def emit_ffn(p, layer):
    io, ld, gd = p.io, p.s_ld, p.s_gd
    stn, sv, ss = p.s_t, p.s_v, p.s_s
    fcw = "fc" if layer == 0 else "w1"
    pjw = "proj" if layer == 0 else "w2"
    XH2 = 0
    HS = 16384
    U1 = 24576
    U2A = 26624
    U2B = 28672
    AB = 29184
    bias_fc = 32 if layer == 0 else 72
    bias_pj = 64 if layer == 0 else 104

    p.LN_XB = 16384
    p.LN_SQ = 20480
    pre = (p.s_t.tok(),) if layer != 0 else (p.s_t.tok(), p.t_qship)
    t_ln = emit_ln(p, [("x", 0, 1024), ("x", 1024, 1024)], XH2, pre_toks=pre)

    BROW, OROW = 256, 2304

    def v_orow(v):
        waitall(v, [t_ln] + ([p.t_wq] if layer == 0 else []))
        v.memset(p.consts[0:1, OROW:OROW + 512], 1.0)
        if layer == 0:
            v.drain()
            v.tensor_scalar(out=p.consts[0:1, OROW:OROW + 512],
                            in0=p.consts[0:1, OROW:OROW + 512],
                            scalar1=p.small[0:1, 36:37], scalar2=None, op0=ALU.mult)
        sv.bump(v.drain())
    p.block.vector(v_orow)
    t_orow = sv.tok()

    def v_badd(v):
        waitall(v, [t_ln])
        for ch in range(NCH):
            v.tensor_scalar(out=p.xs[:, ch * TOK:(ch + 1) * TOK],
                            in0=p.xs[:, ch * TOK:(ch + 1) * TOK],
                            scalar1=p.biases[:, bias_pj + ch:bias_pj + ch + 1],
                            scalar2=None, op0=ALU.add)
        sv.bump(v.drain())
    p.block.vector(v_badd)
    t_badd = sv.tok()

    if layer == 0:
        def g_a(g):
            gd.bump(g.dma_start(out=p.aarena[:, AB:AB + 128], in_=io["AfcT_t"][:, :]))
            gd.bump(g.dma_start(out=p.aarena[:, AB + 128:AB + 640], in_=io["AprojT_t"][:, :]))
            gd.bump(g.dma_start(out=p.warena[0:16, 22528:23552], in_=io["BprojT"][:, :]))
            g.wait_ge(gd.sem, gd.v)
        p.block.gpsimd(g_a)
    t_ab = gd.tok()

    browk = "bfc_row" if layer == 0 else "b1_row"
    for half in range(2):
        def g_fch(g, half=half):
            g.wait_ge(stn.sem, stn.v)
            waitall(g, [p.t_wq])
            for r in range(4 * half, 4 * half + 4):
                for t1 in range(4):
                    src = wq_dram(p, fcw, r).rearrange("q (ch t) -> q ch t", ch=8) \
                        [:, :, t1 * 128:t1 * 128 + 128]
                    oc = 4 * (r - 4 * half) + t1
                    dst = p.warena[:, oc * 1024:(oc + 1) * 1024] \
                        .rearrange("q (ch t0) -> q ch t0", ch=8)
                    gd.bump(g.dma_start(out=dst, in_=src))
            if layer == 0:
                gd.bump(g.dma_start(out=p.warena[0:16, 20480:22528],
                                    in_=io["BfcT"][:, 2048 * half:2048 * (half + 1)]))
            g.wait_ge(gd.sem, gd.v)
        p.block.gpsimd(g_fch)
        t_fch = gd.tok()

        def g_brow(g, half=half):
            g.wait_ge(stn.sem, stn.v)
            waitall(g, [t_ln, t_orow])
            gd.bump(g.dma_start(out=p.consts[0:1, BROW:BROW + 2048],
                                in_=io[browk][0:1, 2048 * half:2048 * (half + 1)]))
            g.wait_ge(gd.sem, gd.v)
        p.block.gpsimd(g_brow)
        t_brow = gd.tok()

        for un in range(4):
            u0 = 512 * un
            if layer == 0 and half == 0:
                def t_u1(t, un=un):
                    if un == 0:
                        waitall(t, [t_ln, t_ab])
                    t.wait_ge(ss.sem, ss.v)
                    for ch in range(NCH):
                        mm = t.matmul(p.psD[0:16, 0:512],
                                      p.aarena[:, AB + ch * 16:AB + (ch + 1) * 16],
                                      p.aarena[:, XH2 + ch * 2048 + u0: XH2 + ch * 2048 + u0 + 512],
                                      start=(ch == 0), stop=(ch == NCH - 1))
                    stn.bump(mm)
                p.block.tensor(t_u1)
                t_u1t = stn.tok()

                def v_u1(v, un=un, u0=u0):
                    waitall(v, [t_u1t])
                    v.tensor_scalar(out=p.aarena[0:16, U1 + u0:U1 + u0 + 512],
                                    in0=p.psD[0:16, 0:512],
                                    scalar1=p.small[0:16, 40:41], scalar2=None,
                                    op0=ALU.mult)
                    sv.bump(v.drain())
                p.block.vector(v_u1)

            fcg_base = stn.v
            gel_base = ss.v
            for fj in range(16):
                phalf = fj % 2

                def t_fc(t, half=half, un=un, u0=u0, fj=fj, phalf=phalf):
                    if fj == 0 and un == 0:
                        waitall(t, [t_fch, t_brow, t_ln, t_ab])
                    if fj >= 2:
                        t.wait_ge(ss.sem, gel_base + fj - 1)
                    else:
                        t.wait_ge(ss.sem, ss.v)
                        t.wait_ge(sv.sem, sv.v)
                    t.matmul(p.psB[:, phalf * 512:phalf * 512 + 512],
                             p.consts[0:1, BROW + fj * 128:BROW + (fj + 1) * 128],
                             p.consts[0:1, OROW:OROW + 512],
                             start=True, stop=False)
                    last = NCH - 1
                    for ch in range(NCH):
                        stop = (layer != 0) and (ch == last)
                        mm = t.matmul(p.psB[:, phalf * 512:phalf * 512 + 512],
                                      p.warena[:, fj * 1024 + ch * 128: fj * 1024 + (ch + 1) * 128],
                                      p.aarena[:, XH2 + ch * 2048 + u0: XH2 + ch * 2048 + u0 + 512],
                                      start=False, stop=stop)
                    if layer == 0:
                        mm = t.matmul(p.psB[:, phalf * 512:phalf * 512 + 512],
                                      p.warena[0:16, 20480 + fj * 128:20480 + (fj + 1) * 128],
                                      p.aarena[0:16, U1 + u0:U1 + u0 + 512],
                                      start=False, stop=True)
                    stn.bump(mm)
                p.block.tensor(t_fc)

            def s_gelh(s, half=half, u0=u0, fcg_base=fcg_base):
                for fj in range(16):
                    s.wait_ge(stn.sem, fcg_base + fj + 1)
                    s.activation(out=p.aarena[0:128, HS + fj * 512:HS + (fj + 1) * 512],
                                 in_=p.psB[:, (fj % 2) * 512:(fj % 2) * 512 + 512],
                                 func=AF.Gelu, bias=0.0,
                                 scale=(p.small[:, 32:33] if layer == 0 else 1.0))
                    s.drain().then_inc(ss.sem, 1)
            p.block.scalar(s_gelh)
            ss.v += 16
            t_hdone = ss.tok()

            if layer == 0:
                def t_u2(t, half=half, u0=u0, t_hdone=t_hdone):
                    waitall(t, [t_hdone])
                    for fj in range(16):
                        ch = 16 * half + fj
                        mm = t.matmul(p.psD[0:16, 0:512],
                                      p.aarena[:, AB + 128 + ch * 16:AB + 128 + (ch + 1) * 16],
                                      p.aarena[:, HS + fj * 512:HS + (fj + 1) * 512],
                                      start=(fj == 0), stop=(fj == 15))
                    stn.bump(mm)
                p.block.tensor(t_u2)
                t_u2t = stn.tok()

                def v_u2(v, half=half, u0=u0, t_u2t=t_u2t):
                    waitall(v, [t_u2t])
                    dstc = (U2A if half == 0 else U2B) + (u0 if half == 0 else 0)
                    v.tensor_scalar(out=p.aarena[0:16, dstc:dstc + 512],
                                    in0=p.psD[0:16, 0:512],
                                    scalar1=p.small[0:16, 41:42], scalar2=None,
                                    op0=ALU.mult)
                    sv.bump(v.drain())
                p.block.vector(v_u2)
            t_u2d = sv.tok()

            for oc in range(8):
                slot = oc % 2

                def sy_pj(sy, half=half, oc=oc, slot=slot):
                    sy.wait_ge(stn.sem, stn.v)
                    src = wq_dram(p, pjw, oc)[:, 16 * 128 * half:16 * 128 * half + 2048]
                    ld.bump(sy.dma_start(out=p.warena[:, 16384 + slot * 2048:16384 + (slot + 1) * 2048],
                                         in_=src))
                p.block.sync(sy_pj)
                t_pjl = ld.tok()

                def t_pj(t, half=half, un=un, u0=u0, oc=oc, slot=slot, t_pjl=t_pjl,
                         t_u2d=t_u2d, t_hdone=t_hdone):
                    waitall(t, [t_pjl, t_u2d, t_hdone])
                    if oc >= 2:
                        t.wait_ge(sv.sem, sv.v)
                    phalf = oc % 2
                    last = 15
                    for fj in range(16):
                        stop = (layer != 0 or half == 0) and (fj == last)
                        mm = t.matmul(p.psB[:, phalf * 512:phalf * 512 + 512],
                                      p.warena[:, 16384 + slot * 2048 + fj * 128:
                                                16384 + slot * 2048 + (fj + 1) * 128],
                                      p.aarena[:, HS + fj * 512:HS + (fj + 1) * 512],
                                      start=(fj == 0), stop=stop)
                    if layer == 0 and half == 1:
                        t.matmul(p.psB[:, phalf * 512:phalf * 512 + 512],
                                 p.warena[0:16, 22528 + oc * 128:22528 + (oc + 1) * 128],
                                 p.aarena[0:16, U2A + u0:U2A + u0 + 512],
                                 start=False, stop=False)
                        mm = t.matmul(p.psB[:, phalf * 512:phalf * 512 + 512],
                                      p.warena[0:16, 22528 + oc * 128:22528 + (oc + 1) * 128],
                                      p.aarena[0:16, U2B:U2B + 512],
                                      start=False, stop=True)
                    stn.bump(mm)
                p.block.tensor(t_pj)
                t_pjm = stn.tok()

                def v_px(v, half=half, un=un, u0=u0, oc=oc, t_pjm=t_pjm):
                    waitall(v, [t_pjm])
                    if half == 0 and un == 0 and oc == 0:
                        waitall(v, [t_badd])
                    phalf = oc % 2
                    if layer == 0:
                        v.scalar_tensor_tensor(
                            out=p.xs[:, oc * TOK + u0: oc * TOK + u0 + 512],
                            in0=p.psB[:, phalf * 512:phalf * 512 + 512],
                            scalar=p.small[:, 33:34],
                            in1=p.xs[:, oc * TOK + u0: oc * TOK + u0 + 512],
                            op0=ALU.mult, op1=ALU.add)
                    else:
                        v.tensor_tensor(
                            out=p.xs[:, oc * TOK + u0: oc * TOK + u0 + 512],
                            in0=p.xs[:, oc * TOK + u0: oc * TOK + u0 + 512],
                            in1=p.psB[:, phalf * 512:phalf * 512 + 512], op=ALU.add)
                    sv.bump(v.drain())
                p.block.vector(v_px)


ROLE_CHUNKS = {0: (0, 1, 6, 7), 1: (2, 3, 4, 5)}
def _l1_local(role, lc):
    return 512 * lc


def emit_l1_weights_early(p):
    io, gd, stn = p.io, p.s_gd, p.s_t

    def g_w1a(g):
        g.wait_ge(stn.sem, stn.v)
        gd.bump(g.dma_start(out=p.warena[:, 23552:27648], in_=io["Wcd_t"][:, :]))
        gd.bump(g.dma_start(out=p.warena[:, 27648:31744], in_=io["Wqu_t"][:, :]))
        gd.bump(g.dma_start(out=p.warena[:, 31744:32256], in_=io["Wkv_t"][:, :]))
        g.wait_ge(gd.sem, gd.v)
    p.block.gpsimd(g_w1a)
    p.t_w1a = gd.tok()


def emit_l1_attn(p):
    io, ld, gd = p.io, p.s_ld, p.s_gd
    stn, sv, ss, scc = p.s_t, p.s_v, p.s_s, p.s_cc

    XH3 = 0
    LAT = 16384
    KDUP = 16384
    VK = 20480
    PT = 22560
    YC = 23584
    ODD = 27680
    KVO = 27680
    VKO = 24576
    MB = 4352

    p.LN_XB = 16384
    p.LN_SQ = 20480
    t_ln = emit_ln(p, [("x", 0, 1024), ("x", 1024, 1024)], XH3, pre_toks=())

    WCD, WQU, WKV, WCP = 23552, 27648, 31744, 0

    def g_w1(g):
        g.wait_ge(stn.sem, stn.v)
        gd.bump(g.dma_start(out=p.warena[:, WCP:WCP + 8192], in_=io["Wcp_t"][:, :]))
        g.wait_ge(gd.sem, gd.v)
    p.block.gpsimd(g_w1)
    t_w = gd.tok()

    def g_mask(g):
        g.wait_ge(sv.sem, sv.v)
        g.memset(p.consts[:, MB:MB + 1024], 1.0)
        g.drain()
        g.affine_select(out=p.consts[:, MB:MB + 1024], in_=p.consts[:, MB:MB + 1024],
                        pattern=[[1, 1024]], compare_op=ALU.is_ge, fill=0.0,
                        base=-384, channel_multiplier=-1)
        p.s_g.bump(g.drain())
    p.block.gpsimd(g_mask)
    t_mk = p.s_g.tok()

    for lc in range(4):
        for un in range(4):
            half = (4 * lc + un) % 2

            def t_lat(t, lc=lc, un=un, half=half):
                if lc == 0 and un == 0:
                    waitall(t, [t_ln, p.t_w1a])
                t.wait_ge(ss.sem, ss.v)
                for ch in range(NCH):
                    mm = t.matmul(p.psB[:, half * 512:half * 512 + 512],
                                  p.warena[:, WCD + lc * 1024 + ch * 128: WCD + lc * 1024 + (ch + 1) * 128],
                                  p.aarena[:, XH3 + ch * 2048 + un * 512: XH3 + ch * 2048 + (un + 1) * 512],
                                  start=(ch == 0), stop=(ch == NCH - 1))
                stn.bump(mm)
            p.block.tensor(t_lat)
            t_lm = stn.tok()

            def s_le(s, lc=lc, un=un, half=half, t_lm=t_lm):
                waitall(s, [t_lm])
                s.activation(out=p.aarena[:, LAT + lc * 2048 + un * 512: LAT + lc * 2048 + (un + 1) * 512],
                             in_=p.psB[:, half * 512:half * 512 + 512],
                             func=AF.Identity, bias=0.0, scale=1.0)
                ss.bump(s.drain())
            p.block.scalar(s_le)
    t_lat_d = ss.tok()

    for un in range(4):
        def t_k(t, un=un):
            t.wait_ge(ss.sem, ss.v)
            for ch in range(4):
                mm = t.matmul(p.psD[0:64, 0:512],
                              p.warena[:, WKV + ch * 128: WKV + ch * 128 + 64],
                              p.aarena[:, LAT + ch * 2048 + un * 512: LAT + ch * 2048 + (un + 1) * 512],
                              start=(ch == 0), stop=(ch == 3))
            stn.bump(mm)
        p.block.tensor(t_k)
        t_km = stn.tok()

        def s_kev(s, un=un, t_km=t_km):
            waitall(s, [t_km])
            s.activation(out=p.aarena[0:64, KVO + un * 512: KVO + (un + 1) * 512],
                         in_=p.psD[0:64, 0:512], func=AF.Identity, bias=0.0, scale=1.0)
            ss.bump(s.drain())
        p.block.scalar(s_kev)
    for kt in range(16):
        def t_v(t, kt=kt):
            t.wait_ge(sv.sem, sv.v)
            t.wait_ge(ss.sem, ss.v)
            for ch in range(4):
                mm = t.matmul(p.psD[:, 0:64],
                              p.aarena[:, LAT + ch * 2048 + kt * 128: LAT + ch * 2048 + (kt + 1) * 128],
                              p.warena[:, WKV + ch * 128 + 64: WKV + ch * 128 + 128],
                              start=(ch == 0), stop=(ch == 3))
            stn.bump(mm)
        p.block.tensor(t_v)
        t_vm = stn.tok()

        def v_vev(v, kt=kt, t_vm=t_vm):
            waitall(v, [t_vm])
            v.tensor_copy(out=p.aarena[:, VKO + kt * 64: VKO + (kt + 1) * 64], in_=p.psD[:, 0:64])
            sv.bump(v.drain())
        p.block.vector(v_vev)
    t_kv = sv.tok()

    def g_kex1(g):
        waitall(g, [t_kv])
        g.wait_ge(ss.sem, ss.v)
        gd.bump(g.dma_start(out=p.cc_kin[0:131072].rearrange("(q c) -> q c", q=64),
                            in_=p.aarena[0:64, KVO:KVO + 2048]))
        gd.bump(g.dma_start(out=p.cc_kin[131072:262144].rearrange("(q c) -> q c", q=128),
                            in_=p.aarena[:, VKO:VKO + 1024]))
        g.wait_ge(gd.sem, gd.v)
        scc.bump(g.collective_compute(
            "AllGather", ALU.bypass,
            replica_groups=[[0, 1], [2, 3], [4, 5], [6, 7]],
            ins=[p.cc_kin[:]], outs=[p.cc_kout[:]]))
        g.wait_ge(scc.sem, scc.v)
    p.block.gpsimd(g_kex1)

    for qc in range(8):
        for un in range(4):
            half = (4 * qc + un) % 2

            def t_q(t, qc=qc, un=un, half=half):
                if qc == 0 and un == 0:
                    waitall(t, [t_lat_d])
                t.wait_ge(ss.sem, ss.v)
                for ch in range(4):
                    mm = t.matmul(p.psB[:, half * 512:half * 512 + 512],
                                  p.warena[:, WQU + qc * 512 + ch * 128: WQU + qc * 512 + (ch + 1) * 128],
                                  p.aarena[:, LAT + ch * 2048 + un * 512: LAT + ch * 2048 + (un + 1) * 512],
                                  start=(ch == 0), stop=(ch == 3))
                stn.bump(mm)
            p.block.tensor(t_q)
            t_qm = stn.tok()

            def s_qe(s, qc=qc, un=un, half=half, t_qm=t_qm):
                waitall(s, [t_qm])
                s.activation(out=p.aarena[:, XH3 + qc * 2048 + un * 512: XH3 + qc * 2048 + (un + 1) * 512],
                             in_=p.psB[:, half * 512:half * 512 + 512],
                             func=AF.Identity, bias=0.0, scale=0.125)
                ss.bump(s.drain())
            p.block.scalar(s_qe)

    t_qd = stn.tok()

    def g_kex2(g):
        waitall(g, [t_qd])
        g.wait_ge(scc.sem, scc.v)
        for (slab, src0, w, dst0) in ((0, 0, 1024, 0), (0, 1024, 1024, 3072),
                                      (1, 0, 2048, 1024)):
            kbase = 262144 * slab
            srcap = p.cc_kout[kbase:kbase + 131072].rearrange("(q c) -> q c", q=64)[:, src0:src0 + w]
            gd.bump(g.dma_start(out=p.aarena[0:64, KDUP + dst0:KDUP + dst0 + w], in_=srcap))
            gd.bump(g.dma_start(out=p.aarena[64:128, KDUP + dst0:KDUP + dst0 + w], in_=srcap))
        for (slab, t0, nkt, gkt0) in ((0, 0, 8, 0), (0, 8, 8, 24), (1, 0, 16, 8)):
            kbase = 262144 * slab + 131072
            srcap = p.cc_kout[kbase:kbase + 131072].rearrange("(q k c) -> q k c", q=128, k=16) \
                [:, t0:t0 + nkt, :]
            dst = p.aarena[:, VK + gkt0 * 65: VK + (gkt0 + nkt) * 65] \
                .rearrange("q (k c) -> q k c", c=65)[:, :, 0:64]
            gd.bump(g.dma_start(out=dst, in_=srcap))
        g.wait_ge(gd.sem, gd.v)
    p.block.gpsimd(g_kex2)
    t_kx = gd.tok()

    def v_vone(v):
        waitall(v, [t_kx])
        v.memset(p.aarena[:, VK:VK + 32 * 65].rearrange("q (k c) -> q k c", c=65)[:, :, 64:65], 1.0)
        sv.bump(v.drain())
    p.block.vector(v_vone)
    t_vo = sv.tok()

    sg = p.s_g

    def passA_hp(lc, hp, first=False):
        u0 = 512 * lc
        nk = {r: 4 * ROLE_CHUNKS[r][lc] + 4 for r in (0, 1)}
        nmax = max(nk.values())
        mm_base = stn.v
        red_base = sv.v
        smax_guard = p.l1_smax_g

        def t_pa(t, lc=lc, hp=hp, u0=u0, nk=nk, nmax=nmax, mm_base=mm_base,
                 red_base=red_base, first=first):
            if first:
                waitall(t, [p.l1_t_vo, p.l1_t_mk])
                t.wait_ge(ss.sem, ss.v)
            with t.register(f"rpa{lc}_{hp}") as r:
                t.reg_load(r, p.rolei[0:1, 0:1])
                for role in (0, 1):
                    ctx = t.If_eq(r, 0) if role == 0 else t.Else()
                    with ctx:
                        for kt in range(nk[role]):
                            if kt >= 1:
                                t.wait_ge(sv.sem, red_base + 2 * kt - 1)
                            else:
                                t.wait_ge(sv.sem, red_base)
                            t.matmul(p.psB[:, 0:512],
                                     p.aarena[0:64, KDUP + kt * 128: KDUP + (kt + 1) * 128],
                                     p.aarena[0:64, XH3 + hp * 2048 + u0: XH3 + hp * 2048 + u0 + 512],
                                     start=True, stop=True, tile_position=(0, 0)
                                     ).then_inc(stn.sem, 1)
                            if kt >= 1:
                                t.wait_ge(sv.sem, red_base + 2 * kt)
                            t.matmul(p.psB[:, 512:1024],
                                     p.aarena[64:128, KDUP + kt * 128: KDUP + (kt + 1) * 128],
                                     p.aarena[64:128, XH3 + hp * 2048 + u0: XH3 + hp * 2048 + u0 + 512],
                                     start=True, stop=True, tile_position=(64, 0)
                                     ).then_inc(stn.sem, 1)
                        if nk[role] < nmax:
                            t.wait_ge(stn.sem, mm_base + 2 * nk[role])
                        for _ in range(2 * (nmax - nk[role])):
                            t.sem_inc(stn.sem, 1)
                t.end_ifs()
        p.block.tensor(t_pa)
        stn.v += 2 * nmax

        def v_red(v, lc=lc, hp=hp, nk=nk, nmax=nmax, mm_base=mm_base,
                  red_base=red_base, smax_guard=smax_guard):
            with v.register(f"rra{lc}_{hp}") as r:
                v.reg_load(r, p.rolei[0:1, 0:1])
                for role in (0, 1):
                    ctx = v.If_eq(r, 0) if role == 0 else v.Else()
                    with ctx:
                        rc0 = 104 + 2 * hp
                        for kt in range(nk[role]):
                            for h in range(2):
                                v.wait_ge(stn.sem, mm_base + 2 * kt + 1 + h)
                                if kt == 0 and h == 0:
                                    v.wait_ge(sg.sem, smax_guard)
                                if kt == 0:
                                    v.reduce_max(out=p.small[:, rc0 + h:rc0 + h + 1],
                                                 in_=p.psB[:, 512 * h:512 * h + 512],
                                                 axis=AX.X)
                                else:
                                    v.reduce_max(out=p.small[:, 60 + h:61 + h],
                                                 in_=p.psB[:, 512 * h:512 * h + 512],
                                                 axis=AX.X)
                                    v.drain()
                                    v.tensor_tensor(out=p.small[:, rc0 + h:rc0 + h + 1],
                                                    in0=p.small[:, rc0 + h:rc0 + h + 1],
                                                    in1=p.small[:, 60 + h:61 + h],
                                                    op=ALU.max)
                                v.drain().then_inc(sv.sem, 1)
                        v.reduce_max(out=p.small[:, 80 + hp:81 + hp],
                                     in_=p.small[:, rc0:rc0 + 2], axis=AX.X)
                        if nk[role] < nmax:
                            v.wait_ge(sv.sem, red_base + 2 * nk[role])
                        for _ in range(2 * (nmax - nk[role])):
                            v.sem_inc(sv.sem, 1)
                        v.drain().then_inc(sv.sem, 1)
                v.end_ifs()
        p.block.vector(v_red)
        sv.v += 2 * nmax + 1

    def emit_shift_lc(t_pamax, t_psD_free):
        def g_smax(g, t_pamax=t_pamax):
            waitall(g, [t_pamax])
            g.reduce_max(out=p.small[0:1, 89:97], in_=p.small[:, 80:88], axis=AX.C)
            sg.bump(g.drain())
        p.block.gpsimd(g_smax)
        p.l1_smax_g = sg.v
        t_mrow = sg.tok()

        def t_b(t, t_mrow=t_mrow, t_psD_free=t_psD_free):
            waitall(t, [t_mrow, t_psD_free])
            stn.bump(t.matmul(p.psD[:, 0:8], p.onesr[0:1, 0:128], p.small[0:1, 89:97],
                              start=True, stop=True))
        p.block.tensor(t_b)
        t_bt = stn.tok()

        def s_n(s, t_bt=t_bt):
            waitall(s, [t_bt])
            s.activation(out=p.small[:, 96:104], in_=p.psD[:, 0:8],
                         func=AF.Identity, bias=0.0, scale=-1.0)
            ss.bump(s.drain())
        p.block.scalar(s_n)
        return ss.tok()

    p.l1_t_vo = t_vo
    p.l1_t_mk = t_mk
    p.l1_smax_g = sg.v
    for hp in range(8):
        passA_hp(0, hp, first=(hp == 0))
    t_bias = emit_shift_lc(sv.tok(), None)

    for lc in range(4):
        u0 = 512 * lc
        nk = {r: 4 * ROLE_CHUNKS[r][lc] + 4 for r in (0, 1)}
        nmax = max(nk.values())
        dg = {r: 4 * ROLE_CHUNKS[r][lc] for r in (0, 1)}

        for hp in range(8):
            pb_mm_base = stn.v
            pb_exp_base = ss.v
            pb_msk_base = sv.v

            pb_sv_base = sv.v

            def t_pb(t, lc=lc, hp=hp, u0=u0, nk=nk, nmax=nmax,
                     pb_mm_base=pb_mm_base, pb_msk_base=pb_msk_base,
                     pb_exp_base=pb_exp_base, pb_sv_base=pb_sv_base, t_bias=t_bias):
                if hp == 0:
                    waitall(t, [t_bias])
                with t.register(f"r1c{lc}_{hp}") as r:
                    t.reg_load(r, p.rolei[0:1, 0:1])
                    for role in (0, 1):
                        ctx = t.If_eq(r, 0) if role == 0 else t.Else()
                        with ctx:
                            for kt in range(nk[role]):
                                for h in range(2):
                                    if kt == 0:
                                        t.wait_ge(sv.sem, pb_sv_base)
                                        t.wait_ge(ss.sem, pb_exp_base)
                                    else:
                                        t.wait_ge(ss.sem, pb_exp_base + 2 * kt - 1 + h)
                                    t.matmul(p.psA[:, 512 * h:512 * h + 512],
                                             p.aarena[64 * h:64 * h + 64,
                                                      KDUP + kt * 128: KDUP + (kt + 1) * 128],
                                             p.aarena[64 * h:64 * h + 64,
                                                      XH3 + hp * 2048 + u0: XH3 + hp * 2048 + u0 + 512],
                                             start=True, stop=True, tile_position=(64 * h, 0)
                                             ).then_inc(stn.sem, 1)
                                    t.wait_ge(sv.sem, pb_msk_base + 2 * kt + 1 + h)
                                    t.matmul(p.psC[0:65, 512 * h:512 * h + 512],
                                             p.aarena[:, VK + kt * 65: VK + (kt + 1) * 65],
                                             p.aarena[:, PT + 512 * h:PT + 512 * h + 512],
                                             start=(kt == 0), stop=(kt == nk[role] - 1)
                                             ).then_inc(stn.sem, 1)
                            t.wait_ge(sv.sem, pb_msk_base + 2 * nmax + 1)
                            t.matmul(p.psD[0:64, 0:512], p.onesr[64:65, 0:64],
                                     p.fsc[64:65, 0:512], start=True, stop=True
                                     ).then_inc(stn.sem, 1)
                            t.wait_ge(sv.sem, pb_msk_base + 2 * nmax + 2)
                            t.matmul(p.psB[0:64, 0:512], p.onesr[64:65, 0:64],
                                     p.fsc[64:65, 512:1024], start=True, stop=True
                                     ).then_inc(stn.sem, 1)
                            if nk[role] < nmax:
                                t.wait_ge(stn.sem, pb_mm_base + 4 * nk[role] + 2)
                            for _ in range(4 * (nmax - nk[role])):
                                t.sem_inc(stn.sem, 1)
                    t.end_ifs()
            p.block.tensor(t_pb)
            stn.v += 4 * nmax + 2

            def s_exp(s, lc=lc, hp=hp, nk=nk, nmax=nmax,
                      pb_mm_base=pb_mm_base, pb_exp_base=pb_exp_base):
                with s.register(f"r1d{lc}_{hp}") as r:
                    s.reg_load(r, p.rolei[0:1, 0:1])
                    for role in (0, 1):
                        ctx = s.If_eq(r, 0) if role == 0 else s.Else()
                        with ctx:
                            for kt in range(nk[role]):
                                for h in range(2):
                                    s.wait_ge(stn.sem, pb_mm_base + 4 * kt + 2 * h + 1)
                                    s.activation(out=p.aarena[:, PT + 512 * h:PT + 512 * h + 512],
                                                 in_=p.psA[:, 512 * h:512 * h + 512],
                                                 func=AF.Exp, bias=p.small[:, 96 + hp:97 + hp],
                                                 scale=1.0)
                                    s.drain().then_inc(ss.sem, 1)
                            if nk[role] < nmax:
                                s.wait_ge(ss.sem, pb_exp_base + 2 * nk[role])
                            for _ in range(2 * (nmax - nk[role])):
                                s.sem_inc(ss.sem, 1)
                    s.end_ifs()
            p.block.scalar(s_exp)
            ss.v += 2 * nmax

            def v_msk(v, lc=lc, hp=hp, nk=nk, nmax=nmax, dg=dg,
                      pb_mm_base=pb_mm_base, pb_exp_base=pb_exp_base,
                      pb_msk_base=pb_msk_base):
                with v.register(f"r1e{lc}_{hp}") as r:
                    v.reg_load(r, p.rolei[0:1, 0:1])
                    for role in (0, 1):
                        ctx = v.If_eq(r, 0) if role == 0 else v.Else()
                        with ctx:
                            for kt in range(nk[role]):
                                for h in range(2):
                                    v.wait_ge(ss.sem, pb_exp_base + 2 * kt + 1 + h)
                                    if kt >= dg[role]:
                                        j = kt - dg[role]
                                        m = p.consts[:, MB + 384 - 128 * j:MB + 896 - 128 * j]
                                        v.tensor_tensor(
                                            out=p.aarena[:, PT + 512 * h:PT + 512 * h + 512],
                                            in0=p.aarena[:, PT + 512 * h:PT + 512 * h + 512],
                                            in1=m, op=ALU.mult)
                                    v.drain().then_inc(sv.sem, 1)
                            v.wait_ge(stn.sem, pb_mm_base + 4 * nk[role] - 1)
                            v.tensor_copy(out=p.fsc[64:65, 0:512], in_=p.psC[64:65, 0:512])
                            v.drain()
                            v.reciprocal(out=p.fsc[64:65, 0:512], in_=p.fsc[64:65, 0:512])
                            v.drain().then_inc(sv.sem, 1)
                            v.wait_ge(stn.sem, pb_mm_base + 4 * nk[role])
                            v.tensor_copy(out=p.fsc[64:65, 512:1024], in_=p.psC[64:65, 512:1024])
                            v.drain()
                            v.reciprocal(out=p.fsc[64:65, 512:1024], in_=p.fsc[64:65, 512:1024])
                            v.drain().then_inc(sv.sem, 1)
                            if nk[role] < nmax:
                                v.wait_ge(sv.sem, pb_msk_base + 2 * nk[role] + 2)
                            for _ in range(2 * (nmax - nk[role])):
                                v.sem_inc(sv.sem, 1)
                    v.end_ifs()
            p.block.vector(v_msk)
            sv.v += 2 * nmax + 2
            t_rp = stn.tok()

            def v_y(v, hp=hp, t_rp=t_rp, gdw=gd.v):
                waitall(v, [t_rp])
                if hp % 4 == 0:
                    v.wait_ge(gd.sem, gdw)
                v.tensor_copy(out=p.aarena[0:64, PT:PT + 512], in_=p.psD[0:64, 0:512])
                v.tensor_copy(out=p.aarena[0:64, PT + 512:PT + 1024], in_=p.psB[0:64, 0:512])
                v.drain()
                v.tensor_tensor(out=p.aarena[0:64, YC + hp * 512:YC + (hp + 1) * 512],
                                in0=p.psC[0:64, 0:512], in1=p.aarena[0:64, PT:PT + 512],
                                op=ALU.mult)
                v.tensor_tensor(out=p.aarena[0:64, ODD + (hp % 4) * 512:ODD + (hp % 4 + 1) * 512],
                                in0=p.psC[0:64, 512:1024], in1=p.aarena[0:64, PT + 512:PT + 1024],
                                op=ALU.mult)
                sv.bump(v.drain())
            p.block.vector(v_y)
            t_yh = sv.tok()

            if hp % 4 == 3:
                def g_yodd(g, hp=hp, t_yh=t_yh):
                    waitall(g, [t_yh])
                    src = p.aarena[0:64, ODD:ODD + 2048].rearrange("q (k c) -> q k c", k=4)
                    dst = p.aarena[64:128, YC + (hp - 3) * 512:YC + (hp + 1) * 512] \
                        .rearrange("q (k c) -> q k c", k=4)
                    gd.bump(g.dma_start(out=dst, in_=src))
                    g.wait_ge(gd.sem, gd.v)
                p.block.gpsimd(g_yodd)

            if lc < 3:
                passA_hp(lc + 1, hp)
        t_yall = gd.tok()
        t_vy7 = (sv, t_yh[1])

        if lc < 3:
            t_bias = emit_shift_lc(sv.tok(), t_vy7)

        vres_toks = []
        for oc in range(8):
            half = oc % 2

            def t_o(t, lc=lc, u0=u0, oc=oc, half=half, t_yall=t_yall, t_vy7=t_vy7,
                    vres_toks=tuple(vres_toks)):
                if oc == 0:
                    waitall(t, [t_yall, t_vy7])
                if oc >= 2:
                    waitall(t, [vres_toks[oc - 2]])
                for ch in range(NCH):
                    mm = t.matmul(p.psA[:, half * 512:half * 512 + 512],
                                  p.warena[:, WCP + oc * 1024 + ch * 128: WCP + oc * 1024 + (ch + 1) * 128],
                                  p.aarena[:, YC + ch * 512: YC + (ch + 1) * 512],
                                  start=(ch == 0), stop=(ch == NCH - 1))
                stn.bump(mm)
            p.block.tensor(t_o)
            t_om = stn.tok()

            def v_res(v, lc=lc, u0=u0, oc=oc, half=half, t_om=t_om):
                waitall(v, [t_om])
                v.tensor_tensor(out=p.xs[:, oc * TOK + u0: oc * TOK + u0 + 512],
                                in0=p.xs[:, oc * TOK + u0: oc * TOK + u0 + 512],
                                in1=p.psA[:, half * 512:half * 512 + 512], op=ALU.add)
                sv.bump(v.drain())
            p.block.vector(v_res)
            vres_toks.append(sv.tok())


def emit_shift(p, t_pamax):
    stn, sv, ss = p.s_t, p.s_v, p.s_s

    def g_smax(g):
        waitall(g, [t_pamax])
        g.reduce_max(out=p.small[0:1, 89:97], in_=p.small[:, 80:88], axis=AX.C)
        p.s_g.bump(g.drain())
    p.block.gpsimd(g_smax)
    t_r2 = p.s_g.tok()

    def t_b(t):
        waitall(t, [t_r2])
        stn.bump(t.matmul(p.psD[:, 0:8], p.onesr[0:1, 0:128], p.small[0:1, 89:97],
                          start=True, stop=True))
    p.block.tensor(t_b)
    t_bt = stn.tok()

    def s_n(s):
        waitall(s, [t_bt])
        s.activation(out=p.small[:, 96:104], in_=p.psD[:, 0:8],
                     func=AF.Identity, bias=0.0, scale=-1.0)
        ss.bump(s.drain())
    p.block.scalar(s_n)
    p.t_shift = ss.tok()


def emit_l1_ffn(p):
    emit_ffn(p, layer=1)


def kernel(**inputs):
    per_core = stage_inputs(inputs)
    nc = build_program()
    res = run_bass_kernel_spmd(nc, per_core, core_ids=list(range(NCORES)))
    return gather_output(res)


# version 10
# speedup vs baseline: 1.1161x; 1.1161x over previous
import numpy as np
import ml_dtypes
from contextlib import ExitStack

import concourse.bass as bass
import concourse.mybir as mybir
from concourse.bass_utils import run_bass_kernel_spmd

F32 = mybir.dt.float32
BF16 = mybir.dt.bfloat16
I32 = mybir.dt.int32
AF = mybir.ActivationFunctionType
ALU = mybir.AluOpType
AX = mybir.AxisListType

D = 1024
S = 4096
B = 4
NCORES = 8
TOK = 2048
NCH = 8
ROLE_BLOCKS = {0: (0, 3), 1: (1, 2)}

STAGE = 4



def _pm(vec, nch):
    return np.ascontiguousarray(np.asarray(vec).reshape(nch, 128).T)


def _tiles(WT):
    IN, OUT = WT.shape
    A = WT.reshape(IN // 128, 128, OUT // 128, 128).transpose(1, 2, 0, 3)
    return np.ascontiguousarray(A.reshape(128, (OUT // 128) * IN))


def stage_inputs(inp):
    x = np.asarray(inp["L0_attn_x"])
    WqkvT = np.asarray(inp["L0_attn_Wqkv"]).T
    WoutT = np.asarray(inp["L0_attn_Wout"]).T
    W_fcT = np.ascontiguousarray(np.asarray(inp["L0_ffn_W_fc"]).T)
    W_projT = np.ascontiguousarray(np.asarray(inp["L0_ffn_W_proj"]).T)
    W1T = np.ascontiguousarray(np.asarray(inp["L1_ffn_W1"]).T)
    W2T = np.ascontiguousarray(np.asarray(inp["L1_ffn_W2"]).T)
    shared = {
        "Wqkv_t": _tiles(WqkvT),
        "Wout_t": _tiles(WoutT),
        "bqkv": _pm(inp["L0_attn_bqkv"], 24),
        "bqkv_row": np.ascontiguousarray(np.asarray(inp["L0_attn_bqkv"])[None, 2048:3072]),
        "bout": _pm(inp["L0_attn_bout"], 8),
        "AfcT_t": np.ascontiguousarray(np.asarray(inp["L0_ffn_A_fc"]).T.reshape(8, 128, 16).transpose(1, 0, 2).reshape(128, 128)),
        "b_fc": _pm(inp["L0_ffn_b_fc"], 32),
        "b_proj": _pm(inp["L0_ffn_b_proj"], 8),
        "Wcd_t": _tiles(np.asarray(inp["L1_attn_Wc_down"])),
        "Wqu_t": _tiles(np.asarray(inp["L1_attn_Wq_up"])),
        "Wkv_t": _tiles(np.asarray(inp["L1_attn_Wkv_up"])),
        "Wcp_t": _tiles(np.asarray(inp["L1_attn_Wc_proj"])),
        "W1b": _pm(inp["L1_ffn_b1"], 32),
        "W2b": _pm(inp["L1_ffn_b2"], 8),
        "BfcT": np.ascontiguousarray(np.asarray(inp["L0_ffn_B_fc"]).T),
        "BprojT": np.ascontiguousarray(np.asarray(inp["L0_ffn_B_proj"]).T),
        "bproj_row": np.ascontiguousarray(np.asarray(inp["L0_ffn_b_proj"])[None, :]),
        "b2_row": np.ascontiguousarray(np.asarray(inp["L1_ffn_b2"])[None, :]),
        "bfc_row": np.ascontiguousarray(np.asarray(inp["L0_ffn_b_fc"])[None, :]),
        "b1_row": np.ascontiguousarray(np.asarray(inp["L1_ffn_b1"])[None, :]),
        "AprojT_t": np.ascontiguousarray(np.asarray(inp["L0_ffn_A_proj"]).T.reshape(32, 128, 16).transpose(1, 0, 2).reshape(128, 512)),
    }
    per_core = []
    for c in range(NCORES):
        s = c // 2
        role = c & 1
        blocks = ROLE_BLOCKS[role]
        xo = np.concatenate([x[s, 1024 * g:1024 * (g + 1), :] for g in blocks], axis=0)
        halos = []
        for g in blocks:
            lo = np.zeros((256, D), np.float32) if g == 0 else x[s, 1024 * g - 256:1024 * g, :]
            hi = np.zeros((256, D), np.float32) if g == 3 else x[s, 1024 * (g + 1):1024 * (g + 1) + 256, :]
            halos += [lo, hi]
        d = dict(shared)
        d["x_own"] = np.ascontiguousarray(xo.T)
        d["x_halo"] = np.ascontiguousarray(np.concatenate(halos, axis=0).T)
        d["role"] = np.array([[role]], np.int32)
        for nm, Wt, w in (("WfcTs", W_fcT, 512), ("W1Ts", W1T, 512)):
            sl = Wt[:, w * c:w * (c + 1)]
            d[nm] = np.ascontiguousarray(sl.reshape(8, 128, w).transpose(1, 0, 2).reshape(128, 8 * w))
        for nm, Wt, w in (("WprojTs", W_projT, 128), ("W2Ts", W2T, 128)):
            sl = Wt[:, w * c:w * (c + 1)]
            d[nm] = np.ascontiguousarray(sl.reshape(32, 128, w).transpose(1, 0, 2).reshape(128, 32 * w))
        per_core.append(d)
    return per_core


def gather_output(res):
    out = np.empty((B, S, D), np.float32)
    for c in range(NCORES):
        s = c // 2
        role = c & 1
        xo = np.asarray(res.results[c]["x_out"]).T
        for i, g in enumerate(ROLE_BLOCKS[role]):
            out[s, 1024 * g:1024 * (g + 1), :] = xo[1024 * i:1024 * (i + 1), :]
    return out



class Ctr:
    def __init__(self, sem, step=1):
        self.sem = sem
        self.step = step
        self.v = 0

    def bump(self, instr):
        instr.then_inc(self.sem, self.step)
        self.v += self.step
        return (self, self.v)

    def tok(self):
        return (self, self.v)


def waitall(eng, toks):
    for t in toks:
        if t is None:
            continue
        c, v = t
        if v > 0:
            eng.wait_ge(c.sem, v)


class P:
    pass


def build_program():
    nc = bass.Bass("TRN2", target_bir_lowering=False, debug=False)
    p = P()
    p.nc = nc

    io = {}

    def din(name, shape, dt=F32):
        io[name] = nc.dram_tensor(name, shape, dt, kind="ExternalInput").ap()

    din("x_own", [D, TOK]); din("x_halo", [D, 1024]); din("role", [1, 1], I32)
    din("Wqkv_t", [128, 24 * 1024]); din("bqkv", [128, 24]); din("bqkv_row", [1, 1024])
    din("Wout_t", [128, 8 * 1024]); din("bout", [128, 8])
    din("WfcTs", [128, 8 * 512]); din("WprojTs", [128, 32 * 128])
    din("W1Ts", [128, 8 * 512]); din("W2Ts", [128, 32 * 128])
    din("AfcT_t", [128, 128]); din("BfcT", [16, 4096])
    din("AprojT_t", [128, 512]); din("BprojT", [16, 1024])
    din("b_fc", [128, 32]); din("b_proj", [128, 8])
    din("bproj_row", [1, 1024]); din("b2_row", [1, 1024])
    din("bfc_row", [1, 4096]); din("b1_row", [1, 4096])
    din("Wcd_t", [128, 4 * 1024]); din("Wqu_t", [128, 8 * 512])
    din("Wkv_t", [128, 512]); din("Wcp_t", [128, 8 * 1024])
    din("W1b", [128, 32]); din("W2b", [128, 8])
    io["x_out"] = nc.dram_tensor("x_out", [D, TOK], F32, kind="ExternalOutput").ap()
    p.io = io

    p.cc_qin = nc.dram_tensor("cc_qin", [4], F32).ap()
    p.cc_qout = nc.dram_tensor("cc_qout", [32], F32).ap()
    p.cc_win = nc.dram_tensor("cc_win", [4 * 512 * 1024], BF16).ap()
    p.cc_wout = nc.dram_tensor("cc_wout", [8 * 4 * 512 * 1024], BF16).ap()
    p.cc_kin = nc.dram_tensor("cc_kin", [64 * TOK + TOK * 64], BF16).ap()
    p.cc_kout = nc.dram_tensor("cc_kout", [2 * (64 * TOK + TOK * 64)], BF16).ap()

    es = ExitStack()
    with es:
        sb = lambda name, shape, dt: es.enter_context(nc.sbuf_tensor(name, shape, dt))
        psf = lambda name, shape, dt: es.enter_context(nc.psum_tensor(name, shape, dt))

        p.xs = sb("xs", [128, NCH * TOK], F32)
        p.warena = sb("warena", [128, 32768], BF16)
        p.aarena = sb("aarena", [128, 29824], BF16)
        p.wsl = sb("wsl", [128, 512], F32)
        p.fsc = sb("fsc", [128, 1024], F32)
        p.vbias = sb("vbias", [1, 1024], BF16)
        p.small = sb("small", [128, 128], F32)
        p.onesr = sb("onesr", [128, 128], F32)
        p.consts = sb("consts", [128, 5380], BF16)
        p.biases = sb("biases", [128, 112], F32)
        p.rolei = sb("rolei", [1, 1], I32)

        p.psA = psf("psA", [128, 1024], F32)
        p.psB = psf("psB", [128, 1024], F32)
        p.psC = psf("psC", [128, 1024], F32)
        p.psD = psf("psD", [128, 512], F32)

        p.s_ld = Ctr(es.enter_context(nc.semaphore("s_ld")), 16)
        p.s_gd = Ctr(es.enter_context(nc.semaphore("s_gd")), 16)
        p.s_t = Ctr(es.enter_context(nc.semaphore("s_t")), 1)
        p.s_v = Ctr(es.enter_context(nc.semaphore("s_v")), 1)
        p.s_s = Ctr(es.enter_context(nc.semaphore("s_s")), 1)
        p.s_g = Ctr(es.enter_context(nc.semaphore("s_g")), 1)
        p.s_cc = Ctr(es.enter_context(nc.semaphore("s_cc")), 1)
        p.s_st = Ctr(es.enter_context(nc.semaphore("s_st")), 16)

        nc.const_aps.register(F32, 1e-5, p.small[:, 4:5])
        p.block = es.enter_context(nc.Block())
        emit_program(p)

    return nc


def emit_program(p):
    io, ld, st = p.io, p.s_ld, p.s_st

    def loads(s):
        for ch in range(NCH):
            ld.bump(s.dma_start(out=p.xs[:, ch * TOK:(ch + 1) * TOK],
                                in_=io["x_own"][128 * ch:128 * (ch + 1), :]))
        ld.bump(s.dma_start(out=p.rolei[:, :], in_=io["role"][:, :]))
        ld.bump(s.dma_start(out=p.biases[:, 0:24], in_=io["bqkv"][:, :]))
        ld.bump(s.dma_start(out=p.biases[:, 24:32], in_=io["bout"][:, :]))
        ld.bump(s.dma_start(out=p.biases[:, 32:64], in_=io["b_fc"][:, :]))
        ld.bump(s.dma_start(out=p.biases[:, 64:72], in_=io["b_proj"][:, :]))
        ld.bump(s.dma_start(out=p.biases[:, 72:104], in_=io["W1b"][:, :]))
        ld.bump(s.dma_start(out=p.biases[:, 104:112], in_=io["W2b"][:, :]))
    p.block.sync(loads)

    def g_vb(g):
        p.s_gd.bump(g.dma_start(out=p.vbias[0:1, :], in_=io["bqkv_row"][0:1, :]))
        g.wait_ge(p.s_gd.sem, p.s_gd.v)
    p.block.gpsimd(g_vb)
    p.t_vb = p.s_gd.tok()
    p.t_base = ld.tok()

    if STAGE == 0:
        def store0(s):
            waitall(s, [p.t_base])
            for ch in range(NCH):
                st.bump(s.dma_start(out=io["x_out"][128 * ch:128 * (ch + 1), :],
                                    in_=p.xs[:, ch * TOK:(ch + 1) * TOK]))
            s.wait_ge(st.sem, st.v)
        p.block.sync(store0)
        return

    with p.nc.named_scope("consts"):
        emit_constants(p)
    with p.nc.named_scope("l0attn"):
        emit_l0_attn(p)
    if STAGE >= 2:
        with p.nc.named_scope("quant"):
            emit_quant(p)
        with p.nc.named_scope("l0ffn"):
            emit_ffn(p, layer=0)
    if STAGE >= 3:
        with p.nc.named_scope("l1attn"):
            emit_l1_attn(p)
    if STAGE >= 4:
        with p.nc.named_scope("l1ffn"):
            emit_ffn(p, layer=1)

    sv, ss = p.s_v, p.s_s

    def storef(s):
        s.wait_ge(sv.sem, sv.v)
        s.wait_ge(ss.sem, ss.v)
        for ch in range(NCH):
            st.bump(s.dma_start(out=io["x_out"][128 * ch:128 * (ch + 1), :],
                                in_=p.xs[:, ch * TOK:(ch + 1) * TOK]))
        s.wait_ge(st.sem, st.v)
    p.block.sync(storef)


def emit_constants(p):
    gr, sv = p.s_g, p.s_v

    def g_consts(g):
        c = p.consts
        g.memset(c[:, 0:5376], 1.0)
        g.drain()
        g.affine_select(out=c[:, 0:128], in_=c[:, 0:128], pattern=[[-1, 128]],
                        compare_op=ALU.is_equal, fill=0.0, base=0, channel_multiplier=1)
        for j in range(8):
            m = c[:, 256 + 512 * j:256 + 512 * (j + 1)]
            g.affine_select(out=m, in_=m, pattern=[[-1, 512]],
                            compare_op=ALU.is_ge, fill=0.0, base=128 * j, channel_multiplier=1)
        g.drain()
        for j in range(8):
            m = c[:, 256 + 512 * j:256 + 512 * (j + 1)]
            g.affine_select(out=m, in_=m, pattern=[[1, 512]],
                            compare_op=ALU.is_ge, fill=0.0, base=511 - 128 * j, channel_multiplier=-1)
        g.affine_select(out=c[:, 4352:5376], in_=c[:, 4352:5376],
                        pattern=[[-1, 16], [0, 64]], compare_op=ALU.is_equal,
                        fill=0.0, base=0, channel_multiplier=1)
        gr.bump(g.drain())
    p.block.gpsimd(g_consts)
    p.t_consts = gr.tok()

    def v_init(v):
        waitall(v, [p.t_base])
        v.memset(p.small[:, 0:2], 1.0)
        v.memset(p.small[:, 4:5], 1e-5)
        v.memset(p.onesr[:, :], 1.0)
        v.drain()
        v.tensor_scalar(out=p.small[:, 8:16], in0=p.biases[:, 0:8],
                        scalar1=0.125, scalar2=None, op0=ALU.mult)
        sv.bump(v.drain())
    p.block.vector(v_init)
    p.t_vinit = sv.tok()

    p.C_ID = 0
    p.C_ONES = 128
    p.C_L0M = 256
    p.C_E16 = 4352
    p.C_SCR = 5376



def emit_ln(p, segs, dst_col, pre_toks=(), unit=512):
    stn, sv, ss = p.s_t, p.s_v, p.s_s
    LN_XB, LN_SQ = p.LN_XB, p.LN_SQ
    total = sum(w for (_, _, w) in segs)
    assert total % unit == 0
    nunit = total // unit
    units = []
    for u in range(nunit):
        lo, hi = u * unit, (u + 1) * unit
        cur, pos = [], 0
        for (kind, col0, w) in segs:
            s0, s1 = max(lo, pos), min(hi, pos + w)
            if s1 > s0:
                cur.append((kind, col0 + (s0 - pos), s1 - s0, s0 - lo))
            pos += w
        units.append(cur)

    def src_ap(kind, col0, w, ch):
        if kind == "x":
            return p.xs[:, col0 + ch * TOK: col0 + ch * TOK + w]
        return p.aarena[:, col0 + ch * 512: col0 + ch * 512 + w]

    for u, segl in enumerate(units):
        def v_cast(v, segl=segl, u=u):
            if u == 0:
                waitall(v, list(pre_toks))
            for ch in range(NCH):
                for (kind, col0, w, off) in segl:
                    v.tensor_copy(out=p.aarena[:, LN_XB + ch * 512 + off: LN_XB + ch * 512 + off + w],
                                  in_=src_ap(kind, col0, w, ch))
            sv.bump(v.drain())
        p.block.vector(v_cast)
        t_cast = sv.tok()

        def s_sq(s):
            waitall(s, [t_cast])
            for ch in range(NCH):
                s.activation(out=p.aarena[:, LN_SQ + ch * 512: LN_SQ + ch * 512 + unit],
                             in_=p.aarena[:, LN_XB + ch * 512: LN_XB + ch * 512 + unit],
                             func=AF.Square)
            ss.bump(s.drain())
        p.block.scalar(s_sq)
        t_sq = ss.tok()

        def t_sums(t):
            waitall(t, [t_sq, p.t_consts])
            for ch in range(NCH):
                t.matmul(p.psA[0:1, 0:unit], p.consts[:, p.C_ONES:p.C_ONES + 1],
                         p.aarena[:, LN_XB + ch * 512: LN_XB + ch * 512 + unit],
                         start=(ch == 0), stop=(ch == NCH - 1))
            for ch in range(NCH):
                mm = t.matmul(p.psA[0:1, 512:512 + unit], p.consts[:, p.C_ONES:p.C_ONES + 1],
                              p.aarena[:, LN_SQ + ch * 512: LN_SQ + ch * 512 + unit],
                              start=(ch == 0), stop=(ch == NCH - 1))
            stn.bump(mm)
        p.block.tensor(t_sums)
        t_sum = stn.tok()

        def v_stats(v):
            waitall(v, [t_sum])
            v.tensor_scalar(out=p.fsc[0:1, 0:unit], in0=p.psA[0:1, 0:unit],
                            scalar1=1.0 / D, scalar2=None, op0=ALU.mult)
            v.tensor_scalar(out=p.fsc[0:1, 512:512 + unit], in0=p.psA[0:1, 512:512 + unit],
                            scalar1=1.0 / D, scalar2=None, op0=ALU.mult)
            v.drain()
            v.tensor_tensor(out=p.psA[0:1, 0:unit], in0=p.fsc[0:1, 0:unit],
                            in1=p.fsc[0:1, 0:unit], op=ALU.mult)
            v.drain()
            v.tensor_tensor(out=p.fsc[0:1, 512:512 + unit], in0=p.fsc[0:1, 512:512 + unit],
                            in1=p.psA[0:1, 0:unit], op=ALU.subtract)
            sv.bump(v.drain())
        p.block.vector(v_stats)
        t_var = sv.tok()

        def s_sqrt(s):
            waitall(s, [t_var])
            s.activation(out=p.fsc[0:1, 512:512 + unit], in_=p.fsc[0:1, 512:512 + unit],
                         func=AF.Sqrt, bias=1e-5, scale=1.0)
            ss.bump(s.drain())
        p.block.scalar(s_sqrt)
        t_sqrt = ss.tok()

        def v_recip(v):
            waitall(v, [t_sqrt])
            v.reciprocal(out=p.fsc[0:1, 512:512 + unit], in_=p.fsc[0:1, 512:512 + unit])
            sv.bump(v.drain())
        p.block.vector(v_recip)
        t_rstd = sv.tok()

        def t_bcast(t):
            waitall(t, [t_rstd])
            t.matmul(p.psB[:, 0:unit], p.onesr[0:1, 0:128], p.fsc[0:1, 0:unit],
                     start=True, stop=True)
            stn.bump(t.matmul(p.psB[:, 512:512 + unit], p.onesr[0:1, 0:128],
                              p.fsc[0:1, 512:512 + unit], start=True, stop=True))
        p.block.tensor(t_bcast)
        t_bc = stn.tok()

        def v_apply(v, segl=segl, u=u):
            waitall(v, [t_bc])
            for ch in range(NCH):
                for (kind, col0, w, off) in segl:
                    v.tensor_tensor(out=p.fsc[:, 0:w], in0=src_ap(kind, col0, w, ch),
                                    in1=p.psB[:, off:off + w], op=ALU.subtract)
                    v.drain()
                    dc = dst_col + ch * total + u * unit + off
                    v.tensor_tensor(out=p.aarena[:, dc:dc + w], in0=p.fsc[:, 0:w],
                                    in1=p.psB[:, 512 + off:512 + off + w], op=ALU.mult)
                    v.drain()
            sv.bump(v.drain())
        p.block.vector(v_apply)
    return sv.tok()



def emit_l0_attn(p):
    io = p.io
    ld, gd = p.s_ld, p.s_gd
    stn, sv, ss = p.s_t, p.s_v, p.s_s

    XHW = 0
    QB = 8192
    KB = 12288
    VP = 20480
    PT = 28800
    LB = 29824
    p.LN_XB = 8192
    p.LN_SQ = 12288
    HALO = 16384

    def g_w(g):
        for part in range(4):
            gd.bump(g.dma_start(out=p.warena[:, part * 6144:(part + 1) * 6144],
                                in_=io["Wqkv_t"][:, part * 6144:(part + 1) * 6144]))
        for part in range(2):
            gd.bump(g.dma_start(out=p.warena[:, 24576 + part * 4096:24576 + (part + 1) * 4096],
                                in_=io["Wout_t"][:, part * 4096:(part + 1) * 4096]))
        g.wait_ge(gd.sem, gd.v)
    p.block.gpsimd(g_w)
    t_w = gd.tok()

    def wtile(oc, ch):
        return p.warena[:, oc * 1024 + ch * 128: oc * 1024 + (ch + 1) * 128]

    def wotile(oc, ch):
        return p.warena[:, 24576 + oc * 1024 + ch * 128: 24576 + oc * 1024 + (ch + 1) * 128]

    for bl in range(2):
        for cq in range(2):
            ck = 2 * bl + cq
            own0 = bl * 1024 + cq * 512
            def g_halo(g, bl=bl, cq=cq):
                g.wait_ge(sv.sem, sv.v)
                g.wait_ge(ss.sem, ss.v)
                for ch in range(NCH):
                    if cq == 0:
                        gd.bump(g.dma_start(
                            out=p.aarena[:, HALO + ch * 512: HALO + ch * 512 + 256],
                            in_=io["x_halo"][128 * ch:128 * (ch + 1), bl * 512: bl * 512 + 256]))
                    else:
                        gd.bump(g.dma_start(
                            out=p.aarena[:, HALO + ch * 512: HALO + ch * 512 + 256],
                            in_=io["x_own"][128 * ch:128 * (ch + 1),
                                            bl * 1024 + 256: bl * 1024 + 512]))
                        gd.bump(g.dma_start(
                            out=p.aarena[:, HALO + ch * 512 + 256: HALO + ch * 512 + 512],
                            in_=io["x_halo"][128 * ch:128 * (ch + 1),
                                             bl * 512 + 256: bl * 512 + 512]))
                g.wait_ge(gd.sem, gd.v)
            p.block.gpsimd(g_halo)
            t_halo = gd.tok()

            if cq == 0:
                segs = [("a", HALO, 256), ("x", bl * 1024, 768)]
            else:
                segs = [("a", HALO, 256), ("x", bl * 1024 + 512, 512), ("a", HALO + 256, 256)]
            t_ln = emit_ln(p, segs, XHW, pre_toks=(p.t_base, t_halo, p.t_vinit, t_w))

            for j in range(8):
                half = j % 2

                def t_q(t, j=j, half=half):
                    if j == 0:
                        waitall(t, [t_ln])
                    t.wait_ge(ss.sem, ss.v)
                    for ch in range(NCH):
                        mm = t.matmul(p.psB[:, half * 512:half * 512 + 512], wtile(j, ch),
                                      p.aarena[:, XHW + ch * 1024 + 256: XHW + ch * 1024 + 768],
                                      start=(ch == 0), stop=(ch == NCH - 1))
                    stn.bump(mm)
                p.block.tensor(t_q)
                t_mm = stn.tok()

                def s_qe(s, j=j, half=half):
                    waitall(s, [t_mm])
                    s.activation(out=p.aarena[:, QB + j * 512: QB + (j + 1) * 512],
                                 in_=p.psB[:, half * 512:half * 512 + 512],
                                 func=AF.Identity, bias=p.small[:, 8 + j:9 + j], scale=0.125)
                    ss.bump(s.drain())
                p.block.scalar(s_qe)

            for j in range(8):
                for un in range(2):
                    half = (2 * j + un) % 2

                    def t_k(t, j=j, un=un, half=half):
                        t.wait_ge(ss.sem, ss.v)
                        for ch in range(NCH):
                            mm = t.matmul(p.psB[:, half * 512:half * 512 + 512], wtile(8 + j, ch),
                                          p.aarena[:, XHW + ch * 1024 + un * 512: XHW + ch * 1024 + (un + 1) * 512],
                                          start=(ch == 0), stop=(ch == NCH - 1))
                        stn.bump(mm)
                    p.block.tensor(t_k)
                    t_mm = stn.tok()

                    def s_ke(s, j=j, un=un, half=half):
                        waitall(s, [t_mm])
                        s.activation(out=p.aarena[:, KB + j * 1024 + un * 512: KB + j * 1024 + (un + 1) * 512],
                                     in_=p.psB[:, half * 512:half * 512 + 512],
                                     func=AF.Identity, bias=p.biases[:, 8 + j:9 + j], scale=1.0)
                        ss.bump(s.drain())
                    p.block.scalar(s_ke)

            for kt in range(8):
                for vh in range(2):
                    half = (2 * kt + vh) % 2

                    def t_v(t, kt=kt, vh=vh, half=half):
                        t.wait_ge(sv.sem, sv.v)
                        t.matmul(p.psA[:, half * 512:half * 512 + 512],
                                 p.consts[0:1, p.C_ONES:p.C_ONES + 128],
                                 p.vbias[0:1, vh * 512:(vh + 1) * 512], start=True, stop=False)
                        for ch in range(NCH):
                            base = (16 + 4 * vh) * 1024 + ch * 128
                            rhs = p.warena[:, base:base + 4096] \
                                .rearrange("p (o c) -> p o c", o=4)[:, :, 0:128]
                            mm = t.matmul(p.psA[:, half * 512:half * 512 + 512],
                                          p.aarena[:, XHW + ch * 1024 + kt * 128: XHW + ch * 1024 + (kt + 1) * 128],
                                          rhs, start=False, stop=(ch == NCH - 1))
                        stn.bump(mm)
                    p.block.tensor(t_v)
                    t_mm = stn.tok()

                    def v_ve(v, kt=kt, vh=vh, half=half):
                        waitall(v, [t_mm])
                        dst = p.aarena[:, VP + kt * 1040 + vh * 8 * 65: VP + kt * 1040 + (vh + 1) * 8 * 65]
                        v.tensor_copy(out=dst.rearrange("p (h c) -> p h c", h=8)[:, :, 0:64],
                                      in_=p.psA[:, half * 512:half * 512 + 512].rearrange("p (h c) -> p h c", h=8))
                        sv.bump(v.drain())
                    p.block.vector(v_ve)

            def v_vones(v, ck=ck):
                v.wait_ge(sv.sem, sv.v)
                v.wait_ge(ss.sem, ss.v)
                on = p.aarena[:, VP:VP + 8 * 1040]
                v.memset(on.rearrange("p (k h c) -> p (k h) c", k=8, h=16)[:, :, 64:65], 1.0)
                v.drain()
                with v.register(f"role{ck}") as r:
                    v.reg_load(r, p.rolei[0:1, 0:1])
                    with v.If_eq(r, 0):
                        if ck == 0:
                            for j in range(8):
                                v.memset(p.aarena[:, KB + j * 1024:KB + j * 1024 + 256], 0.0)
                            for kt in range(2):
                                vv = p.aarena[:, VP + kt * 1040: VP + (kt + 1) * 1040]
                                v.memset(vv.rearrange("p (h c) -> p h c", h=16)[:, :, 0:64], 0.0)
                        elif ck == 3:
                            for j in range(8):
                                v.memset(p.aarena[:, KB + j * 1024 + 768:KB + (j + 1) * 1024], 0.0)
                            for kt in range(6, 8):
                                vv = p.aarena[:, VP + kt * 1040: VP + (kt + 1) * 1040]
                                v.memset(vv.rearrange("p (h c) -> p h c", h=16)[:, :, 0:64], 0.0)
                        else:
                            v.nop()
                    with v.Else():
                        v.nop()
                    v.end_ifs()
                sv.bump(v.drain())
            p.block.vector(v_vones)
            t_kv = sv.tok()

            red_toks = []
            for hp in range(8):
                for kt in range(8):
                    pa = p.psA if kt % 2 == 0 else p.psB

                    def t_pa(t, hp=hp, kt=kt, pa=pa):
                        if hp == 0 and kt == 0:
                            waitall(t, [t_kv])
                            t.wait_ge(ss.sem, ss.v)
                        idx = hp * 8 + kt
                        if idx >= 2:
                            waitall(t, [red_toks[idx - 2]])
                        t.matmul(pa[:, 0:512],
                                 p.aarena[0:64, KB + hp * 1024 + kt * 128: KB + hp * 1024 + (kt + 1) * 128],
                                 p.aarena[0:64, QB + hp * 512: QB + (hp + 1) * 512],
                                 start=True, stop=True, tile_position=(0, 0))
                        stn.bump(t.matmul(pa[:, 512:1024],
                                          p.aarena[64:128, KB + hp * 1024 + kt * 128: KB + hp * 1024 + (kt + 1) * 128],
                                          p.aarena[64:128, QB + hp * 512: QB + (hp + 1) * 512],
                                          start=True, stop=True, tile_position=(64, 0)))
                    p.block.tensor(t_pa)
                    t_pamm = stn.tok()

                    def v_red(v, hp=hp, kt=kt, pa=pa):
                        waitall(v, [t_pamm])
                        v.reduce_max(out=p.small[:, 16 + hp * 8 + kt:17 + hp * 8 + kt],
                                     in_=pa[:, 0:1024], axis=AX.X)
                        sv.bump(v.drain())
                    p.block.vector(v_red)
                    red_toks.append(sv.tok())

            def g_pmax(g):
                g.wait_ge(sv.sem, sv.v)
                for hp in range(8):
                    g.reduce_max(out=p.small[0:1, 89 + hp:90 + hp],
                                 in_=p.small[:, 16 + 8 * hp:24 + 8 * hp],
                                 axis=AX.XYZWC)
                p.s_g.bump(g.drain())
            p.block.gpsimd(g_pmax)
            t_mrow = p.s_g.tok()

            def t_mxb(t):
                waitall(t, [t_mrow])
                stn.bump(t.matmul(p.psD[:, 0:8], p.onesr[0:1, 0:128], p.small[0:1, 89:97],
                                  start=True, stop=True))
            p.block.tensor(t_mxb)
            t_mxb_t = stn.tok()

            def s_mxe(s):
                waitall(s, [t_mxb_t])
                s.activation(out=p.small[:, 96:104], in_=p.psD[:, 0:8],
                             func=AF.Identity, bias=0.0, scale=-1.0)
                ss.bump(s.drain())
            p.block.scalar(s_mxe)
            t_bias = ss.tok()

            for hp in range(8):
                mm_base = stn.v
                exp_base = ss.v
                msk_base = sv.v
                sv_prev = sv.v

                def t_pbh(t, hp=hp, mm_base=mm_base, msk_base=msk_base,
                          sv_prev=sv_prev):
                    if hp == 0:
                        waitall(t, [t_bias])
                    for kt in range(8):
                        if kt == 0:
                            t.wait_ge(sv.sem, sv_prev)
                        else:
                            t.wait_ge(sv.sem, msk_base + kt)
                        t.matmul(p.psA[:, 0:512],
                                 p.aarena[0:64, KB + hp * 1024 + kt * 128: KB + hp * 1024 + (kt + 1) * 128],
                                 p.aarena[0:64, QB + hp * 512: QB + (hp + 1) * 512],
                                 start=True, stop=True, tile_position=(0, 0))
                        t.matmul(p.psA[:, 512:1024],
                                 p.aarena[64:128, KB + hp * 1024 + kt * 128: KB + hp * 1024 + (kt + 1) * 128],
                                 p.aarena[64:128, QB + hp * 512: QB + (hp + 1) * 512],
                                 start=True, stop=True, tile_position=(64, 0)
                                 ).then_inc(stn.sem, 1)
                        t.wait_ge(sv.sem, msk_base + kt + 1)
                        t.matmul(p.psC[0:65, 0:512],
                                 p.aarena[:, VP + kt * 1040 + (2 * hp) * 65: VP + kt * 1040 + (2 * hp + 1) * 65],
                                 p.aarena[:, PT:PT + 512],
                                 start=(kt == 0), stop=(kt == 7))
                        t.matmul(p.psC[0:65, 512:1024],
                                 p.aarena[:, VP + kt * 1040 + (2 * hp + 1) * 65: VP + kt * 1040 + (2 * hp + 2) * 65],
                                 p.aarena[:, PT + 512:PT + 1024],
                                 start=(kt == 0), stop=(kt == 7)
                                 ).then_inc(stn.sem, 1)
                p.block.tensor(t_pbh)
                stn.v += 16
                t_avd = stn.tok()

                def s_exph(s, hp=hp, mm_base=mm_base):
                    for kt in range(8):
                        s.wait_ge(stn.sem, mm_base + 2 * kt + 1)
                        s.activation(out=p.aarena[:, PT:PT + 1024], in_=p.psA[:, 0:1024],
                                     func=AF.Exp, bias=p.small[:, 96 + hp:97 + hp], scale=1.0)
                        s.drain().then_inc(ss.sem, 1)
                p.block.scalar(s_exph)
                ss.v += 8

                def v_mskh(v, hp=hp, exp_base=exp_base):
                    for kt in range(8):
                        v.wait_ge(ss.sem, exp_base + kt + 1)
                        m = p.consts[:, p.C_L0M + kt * 512: p.C_L0M + (kt + 1) * 512]
                        v.tensor_tensor(out=p.aarena[:, PT:PT + 512], in0=p.aarena[:, PT:PT + 512],
                                        in1=m, op=ALU.mult)
                        v.tensor_tensor(out=p.aarena[:, PT + 512:PT + 1024],
                                        in0=p.aarena[:, PT + 512:PT + 1024], in1=m, op=ALU.mult)
                        v.drain().then_inc(sv.sem, 1)
                p.block.vector(v_mskh)
                sv.v += 8

                def v_lrec(v, hp=hp):
                    waitall(v, [t_avd])
                    v.tensor_copy(out=p.fsc[64:65, 0:1024], in_=p.psC[64:65, 0:1024])
                    v.drain()
                    v.reciprocal(out=p.fsc[64:65, 0:1024], in_=p.fsc[64:65, 0:1024])
                    sv.bump(v.drain())
                p.block.vector(v_lrec)
                t_rb = sv.tok()

                def t_rbc(t, hp=hp):
                    waitall(t, [t_rb])
                    t.matmul(p.psD[0:64, 0:512], p.onesr[64:65, 0:64],
                             p.fsc[64:65, 0:512], start=True, stop=True)
                    stn.bump(t.matmul(p.psB[0:64, 0:512], p.onesr[64:65, 0:64],
                                      p.fsc[64:65, 512:1024], start=True, stop=True))
                p.block.tensor(t_rbc)
                t_rp = stn.tok()

                def v_y(v, hp=hp):
                    waitall(v, [t_rp])
                    v.tensor_copy(out=p.aarena[0:64, PT:PT + 512], in_=p.psD[0:64, 0:512])
                    v.tensor_copy(out=p.aarena[0:64, PT + 512:PT + 1024], in_=p.psB[0:64, 0:512])
                    v.drain()
                    v.tensor_tensor(out=p.aarena[0:64, XHW + (2 * hp) * 512: XHW + (2 * hp) * 512 + 512],
                                    in0=p.psC[0:64, 0:512], in1=p.aarena[0:64, PT:PT + 512], op=ALU.mult)
                    v.tensor_tensor(out=p.aarena[0:64, XHW + (2 * hp + 1) * 512: XHW + (2 * hp + 1) * 512 + 512],
                                    in0=p.psC[0:64, 512:1024], in1=p.aarena[0:64, PT + 512:PT + 1024], op=ALU.mult)
                    sv.bump(v.drain())
                p.block.vector(v_y)
            t_yall = sv.tok()

            def g_yasm(g):
                waitall(g, [t_yall])
                src = p.aarena[0:64, XHW:XHW + 8192].rearrange("p (h two c) -> p h two c", two=2, c=512)
                dst_e = p.aarena[0:64, QB:QB + 4096].rearrange("p (h c) -> p h c", c=512)
                gd.bump(g.dma_start(out=dst_e, in_=src[:, :, 0:1, :].rearrange("p h two c -> p (h two) c")))
                dst_o = p.aarena[64:128, QB:QB + 4096].rearrange("p (h c) -> p h c", c=512)
                gd.bump(g.dma_start(out=dst_o, in_=src[:, :, 1:2, :].rearrange("p h two c -> p (h two) c")))
                g.wait_ge(gd.sem, gd.v)
            p.block.gpsimd(g_yasm)
            t_yq = gd.tok()

            for oc in range(8):
                half = oc % 2

                def t_o(t, oc=oc, half=half):
                    if oc == 0:
                        waitall(t, [t_yq])
                    if oc >= 2:
                        t.wait_ge(sv.sem, sv.v)
                    for ch in range(NCH):
                        mm = t.matmul(p.psB[:, half * 512:half * 512 + 512], wotile(oc, ch),
                                      p.aarena[:, QB + ch * 512: QB + (ch + 1) * 512],
                                      start=(ch == 0), stop=(ch == NCH - 1))
                    stn.bump(mm)
                p.block.tensor(t_o)
                t_om = stn.tok()

                def v_res(v, oc=oc, half=half, own0=own0):
                    waitall(v, [t_om])
                    v.scalar_tensor_tensor(
                        out=p.xs[:, oc * TOK + own0: oc * TOK + own0 + 512],
                        in0=p.psB[:, half * 512:half * 512 + 512],
                        scalar=p.biases[:, 24 + oc:25 + oc],
                        in1=p.xs[:, oc * TOK + own0: oc * TOK + own0 + 512],
                        op0=ALU.add, op1=ALU.add)
                    sv.bump(v.drain())
                p.block.vector(v_res)


def emit_l0_ffn(p):
    emit_quant(p)
    emit_ffn(p, layer=0)


def emit_quant(p):
    io, ld, gd = p.io, p.s_ld, p.s_gd
    stn, sv, ss, scc = p.s_t, p.s_v, p.s_s, p.s_cc
    WQ = {"fc": 0, "proj": 4096, "w1": 8192, "w2": 12288}
    SRC = {"fc": "WfcTs", "proj": "WprojTs", "w1": "W1Ts", "w2": "W2Ts"}

    for wi, wn in enumerate(["fc", "proj", "w1", "w2"]):
        for sub in range(8):
            def s_load(s, wn=wn, sub=sub):
                s.wait_ge(sv.sem, sv.v)
                ld.bump(s.dma_start(out=p.wsl[:, 0:512],
                                    in_=io[SRC[wn]][:, 512 * sub:512 * (sub + 1)]))
            p.block.sync(s_load)
            t_l = ld.tok()

            def v_red(v, wn=wn, sub=sub, t_l=t_l):
                waitall(v, [t_l])
                if wn in ("fc", "proj"):
                    v.reduce_max(out=p.small[:, 48 + sub:49 + sub], in_=p.wsl[:, 0:512],
                                 axis=AX.X, apply_absolute_value=True)
                else:
                    v.reduce_sum(out=p.small[:, 48 + sub:49 + sub], in_=p.wsl[:, 0:512],
                                 axis=AX.X, apply_absolute_value=True)
                sv.bump(v.drain())
            p.block.vector(v_red)

        def v_comb(v, wi=wi, wn=wn):
            v.wait_ge(sv.sem, sv.v)
            if wn in ("fc", "proj"):
                v.reduce_max(out=p.small[:, 56 + wi:57 + wi], in_=p.small[:, 48:56], axis=AX.X)
            else:
                v.reduce_sum(out=p.small[:, 56 + wi:57 + wi], in_=p.small[:, 48:56], axis=AX.X)
            sv.bump(v.drain())
        p.block.vector(v_comb)
    t_part = sv.tok()

    def g_gather(g):
        waitall(g, [t_part])
        for wi in range(2):
            gd.bump(g.dma_start(out=p.fsc[0:1, 128 * wi:128 * (wi + 1)],
                                in_=p.small[:, 56 + wi:57 + wi]))
        g.wait_ge(gd.sem, gd.v)
    p.block.gpsimd(g_gather)
    t_gath = gd.tok()

    def v_cross(v):
        waitall(v, [t_gath])
        v.reduce_max(out=p.small[0:1, 20:21], in_=p.fsc[0:1, 0:128], axis=AX.X)
        v.reduce_max(out=p.small[0:1, 21:22], in_=p.fsc[0:1, 128:256], axis=AX.X)
        sv.bump(v.drain())
    p.block.vector(v_cross)
    t_vx = sv.tok()

    def t_cross(t):
        waitall(t, [t_part])
        t.matmul(p.psD[0:1, 0:1], p.onesr[:, 0:1], p.small[:, 58:59], start=True, stop=True)
        stn.bump(t.matmul(p.psD[0:1, 1:2], p.onesr[:, 0:1], p.small[:, 59:60],
                          start=True, stop=True))
    p.block.tensor(t_cross)
    t_tx = stn.tok()

    def s_sums(s):
        waitall(s, [t_tx])
        s.activation(out=p.small[0:1, 22:24], in_=p.psD[0:1, 0:2], func=AF.Identity,
                     bias=0.0, scale=1.0)
        ss.bump(s.drain())
    p.block.scalar(s_sums)
    t_sx = ss.tok()

    def g_ag1(g):
        waitall(g, [t_vx, t_sx])
        gd.bump(g.dma_start(out=p.cc_qin[0:4], in_=p.small[0:1, 20:24]))
        g.wait_ge(gd.sem, gd.v)
        scc.bump(g.collective_compute(
            "AllGather", ALU.bypass, replica_groups=[list(range(NCORES))],
            ins=[p.cc_qin[:]], outs=[p.cc_qout[:]]))
        g.wait_ge(scc.sem, scc.v)
        gd.bump(g.dma_start(out=p.fsc[0:1, 0:32], in_=p.cc_qout[:]))
        g.wait_ge(gd.sem, gd.v)
    p.block.gpsimd(g_ag1)
    t_ag1 = gd.tok()

    def v_scales(v):
        waitall(v, [t_ag1])
        r8 = p.fsc[0:1, 0:32].rearrange("p (r k) -> p k r", r=8)
        v.reduce_max(out=p.small[0:1, 24:25], in_=r8[:, 0:1, :], axis=AX.X)
        v.reduce_max(out=p.small[0:1, 25:26], in_=r8[:, 1:2, :], axis=AX.X)
        v.reduce_sum(out=p.small[0:1, 26:27], in_=r8[:, 2:3, :], axis=AX.X)
        v.reduce_sum(out=p.small[0:1, 27:28], in_=r8[:, 3:4, :], axis=AX.X)
        v.drain()
        v.tensor_scalar(out=p.small[0:1, 24:26], in0=p.small[0:1, 24:26],
                        scalar1=1.0 / 127.0, scalar2=None, op0=ALU.mult)
        v.tensor_scalar(out=p.small[0:1, 26:28], in0=p.small[0:1, 26:28],
                        scalar1=1.0 / (4096.0 * 1024.0), scalar2=1e-8,
                        op0=ALU.mult, op1=ALU.max)
        v.drain()
        v.reciprocal(out=p.small[0:1, 28:32], in_=p.small[0:1, 24:28])
        sv.bump(v.drain())
    p.block.vector(v_scales)
    t_sc = sv.tok()

    def t_bc(t):
        waitall(t, [t_sc])
        stn.bump(t.matmul(p.psD[:, 0:8], p.onesr[0:1, 0:128], p.small[0:1, 24:32],
                          start=True, stop=True))
    p.block.tensor(t_bc)
    t_bc_t = stn.tok()

    def s_bc(s):
        waitall(s, [t_bc_t])
        s.activation(out=p.small[:, 32:40], in_=p.psD[:, 0:8], func=AF.Identity,
                     bias=0.0, scale=1.0)
        ss.bump(s.drain())
    p.block.scalar(s_bc)
    t_bcc = ss.tok()

    def v_2rs(v):
        waitall(v, [t_bcc])
        v.tensor_scalar(out=p.small[:, 40:42], in0=p.small[:, 36:38],
                        scalar1=2.0, scalar2=None, op0=ALU.mult)
        sv.bump(v.drain())
    p.block.vector(v_2rs)
    t_2rs = sv.tok()

    for wi, wn in enumerate(["fc", "proj", "w1", "w2"]):
        for sub in range(8):
            def s_load2(s, wn=wn, sub=sub):
                s.wait_ge(sv.sem, sv.v)
                ld.bump(s.dma_start(out=p.wsl[:, 0:512],
                                    in_=io[SRC[wn]][:, 512 * sub:512 * (sub + 1)]))
            p.block.sync(s_load2)
            t_l = ld.tok()
            dst = p.aarena[:, WQ[wn] + 512 * sub: WQ[wn] + 512 * (sub + 1)]
            rs_col = p.small[:, 36 + wi:37 + wi]

            if wn in ("fc", "proj"):
                def v_fq(v, rs_col=rs_col, dst=dst, t_l=t_l):
                    waitall(v, [t_l, t_2rs])
                    z = p.fsc[:, 0:512]
                    v.tensor_scalar(out=z, in0=p.wsl[:, 0:512], scalar1=rs_col,
                                    scalar2=128.5, op0=ALU.mult, op1=ALU.add)
                    v.drain()
                    v.tensor_scalar(out=z, in0=z, scalar1=255.99, scalar2=0.5,
                                    op0=ALU.min, op1=ALU.max)
                    v.drain()
                    v.tensor_copy(out=p.fsc[:, 512:1024], in_=z)
                    v.drain()
                    for bit in (128.0, 64.0, 32.0, 16.0, 8.0, 4.0, 2.0, 1.0):
                        v.tensor_scalar(out=p.psD[:, 0:512], in0=z, scalar1=bit,
                                        scalar2=bit, op0=ALU.is_ge, op1=ALU.mult)
                        v.drain()
                        v.tensor_tensor(out=z, in0=z, in1=p.psD[:, 0:512], op=ALU.subtract)
                        v.drain()
                    v.tensor_tensor(out=p.fsc[:, 512:1024], in0=p.fsc[:, 512:1024],
                                    in1=z, op=ALU.subtract)
                    v.drain()
                    v.tensor_scalar(out=dst, in0=p.fsc[:, 512:1024], scalar1=-128.0,
                                    scalar2=None, op0=ALU.add)
                    sv.bump(v.drain())
                p.block.vector(v_fq)
            else:
                def v_tn(v, rs_col=rs_col, dst=dst, t_l=t_l):
                    waitall(v, [t_l, t_2rs])
                    v.tensor_scalar(out=p.fsc[:, 0:512], in0=p.wsl[:, 0:512],
                                    scalar1=rs_col, scalar2=0.5, op0=ALU.mult, op1=ALU.is_gt)
                    v.tensor_scalar(out=p.fsc[:, 512:1024], in0=p.wsl[:, 0:512],
                                    scalar1=rs_col, scalar2=-0.5, op0=ALU.mult, op1=ALU.is_lt)
                    v.drain()
                    v.tensor_tensor(out=dst, in0=p.fsc[:, 0:512], in1=p.fsc[:, 512:1024],
                                    op=ALU.subtract)
                    sv.bump(v.drain())
                p.block.vector(v_tn)
    t_q = sv.tok()

    def g_ship(g):
        waitall(g, [t_q])
        for wi, wn in enumerate(["fc", "proj", "w1", "w2"]):
            gd.bump(g.dma_start(
                out=p.cc_win[wi * 524288:(wi + 1) * 524288].rearrange("(q c) -> q c", q=128),
                in_=p.aarena[:, WQ[wn]:WQ[wn] + 4096]))
        g.wait_ge(gd.sem, gd.v)
        scc.bump(g.collective_compute(
            "AllGather", ALU.bypass, replica_groups=[list(range(NCORES))],
            ins=[p.cc_win[:]], outs=[p.cc_wout[:]]))
        g.wait_ge(scc.sem, scc.v)
    p.block.gpsimd(g_ship)
    p.t_qship = p.s_gd.tok()
    p.t_wq = p.s_cc.tok()


def wq_dram(p, wn, rank):
    wi = {"fc": 0, "proj": 1, "w1": 2, "w2": 3}[wn]
    base = rank * 4 * 524288 + wi * 524288
    return p.cc_wout[base:base + 524288].rearrange("(q c) -> q c", q=128)


def emit_ffn(p, layer):
    io, ld, gd = p.io, p.s_ld, p.s_gd
    stn, sv, ss = p.s_t, p.s_v, p.s_s
    fcw = "fc" if layer == 0 else "w1"
    pjw = "proj" if layer == 0 else "w2"
    XH2 = 0
    HS = 16384
    U1 = 24576
    U2A = 26624
    U2B = 28672
    AB = 29184
    bias_fc = 32 if layer == 0 else 72
    bias_pj = 64 if layer == 0 else 104

    p.LN_XB = 16384
    p.LN_SQ = 20480
    pre = (p.s_t.tok(),) if layer != 0 else (p.s_t.tok(), p.t_qship)
    t_ln = emit_ln(p, [("x", 0, 1024), ("x", 1024, 1024)], XH2, pre_toks=pre)

    BROW, OROW = 256, 2304

    def v_orow(v):
        waitall(v, [t_ln] + ([p.t_wq] if layer == 0 else []))
        v.memset(p.consts[0:1, OROW:OROW + 512], 1.0)
        if layer == 0:
            v.drain()
            v.tensor_scalar(out=p.consts[0:1, OROW:OROW + 512],
                            in0=p.consts[0:1, OROW:OROW + 512],
                            scalar1=p.small[0:1, 36:37], scalar2=None, op0=ALU.mult)
        sv.bump(v.drain())
    p.block.vector(v_orow)
    t_orow = sv.tok()

    def v_badd(v):
        waitall(v, [t_ln])
        for ch in range(NCH):
            v.tensor_scalar(out=p.xs[:, ch * TOK:(ch + 1) * TOK],
                            in0=p.xs[:, ch * TOK:(ch + 1) * TOK],
                            scalar1=p.biases[:, bias_pj + ch:bias_pj + ch + 1],
                            scalar2=None, op0=ALU.add)
        sv.bump(v.drain())
    p.block.vector(v_badd)
    t_badd = sv.tok()

    if layer == 0:
        def g_a(g):
            gd.bump(g.dma_start(out=p.aarena[:, AB:AB + 128], in_=io["AfcT_t"][:, :]))
            gd.bump(g.dma_start(out=p.aarena[:, AB + 128:AB + 640], in_=io["AprojT_t"][:, :]))
            gd.bump(g.dma_start(out=p.warena[0:16, 22528:23552], in_=io["BprojT"][:, :]))
            g.wait_ge(gd.sem, gd.v)
        p.block.gpsimd(g_a)
    t_ab = gd.tok()

    browk = "bfc_row" if layer == 0 else "b1_row"
    for half in range(2):
        def g_fch(g, half=half):
            g.wait_ge(stn.sem, stn.v)
            waitall(g, [p.t_wq])
            for r in range(4 * half, 4 * half + 4):
                for t1 in range(4):
                    src = wq_dram(p, fcw, r).rearrange("q (ch t) -> q ch t", ch=8) \
                        [:, :, t1 * 128:t1 * 128 + 128]
                    oc = 4 * (r - 4 * half) + t1
                    dst = p.warena[:, oc * 1024:(oc + 1) * 1024] \
                        .rearrange("q (ch t0) -> q ch t0", ch=8)
                    gd.bump(g.dma_start(out=dst, in_=src))
            if layer == 0:
                gd.bump(g.dma_start(out=p.warena[0:16, 20480:22528],
                                    in_=io["BfcT"][:, 2048 * half:2048 * (half + 1)]))
            g.wait_ge(gd.sem, gd.v)
        p.block.gpsimd(g_fch)
        t_fch = gd.tok()

        def g_brow(g, half=half):
            g.wait_ge(stn.sem, stn.v)
            waitall(g, [t_ln, t_orow])
            gd.bump(g.dma_start(out=p.consts[0:1, BROW:BROW + 2048],
                                in_=io[browk][0:1, 2048 * half:2048 * (half + 1)]))
            g.wait_ge(gd.sem, gd.v)
        p.block.gpsimd(g_brow)
        t_brow = gd.tok()

        for un in range(4):
            u0 = 512 * un
            if layer == 0 and half == 0:
                def t_u1(t, un=un):
                    if un == 0:
                        waitall(t, [t_ln, t_ab])
                    t.wait_ge(ss.sem, ss.v)
                    for ch in range(NCH):
                        mm = t.matmul(p.psD[0:16, 0:512],
                                      p.aarena[:, AB + ch * 16:AB + (ch + 1) * 16],
                                      p.aarena[:, XH2 + ch * 2048 + u0: XH2 + ch * 2048 + u0 + 512],
                                      start=(ch == 0), stop=(ch == NCH - 1))
                    stn.bump(mm)
                p.block.tensor(t_u1)
                t_u1t = stn.tok()

                def v_u1(v, un=un, u0=u0):
                    waitall(v, [t_u1t])
                    v.tensor_scalar(out=p.aarena[0:16, U1 + u0:U1 + u0 + 512],
                                    in0=p.psD[0:16, 0:512],
                                    scalar1=p.small[0:16, 40:41], scalar2=None,
                                    op0=ALU.mult)
                    sv.bump(v.drain())
                p.block.vector(v_u1)

            fcg_base = stn.v
            gel_base = ss.v
            for fj in range(16):
                phalf = fj % 2

                def t_fc(t, half=half, un=un, u0=u0, fj=fj, phalf=phalf):
                    if fj == 0 and un == 0:
                        waitall(t, [t_fch, t_brow, t_ln, t_ab])
                    if fj >= 2:
                        t.wait_ge(ss.sem, gel_base + fj - 1)
                    else:
                        t.wait_ge(ss.sem, ss.v)
                        t.wait_ge(sv.sem, sv.v)
                    t.matmul(p.psB[:, phalf * 512:phalf * 512 + 512],
                             p.consts[0:1, BROW + fj * 128:BROW + (fj + 1) * 128],
                             p.consts[0:1, OROW:OROW + 512],
                             start=True, stop=False)
                    last = NCH - 1
                    for ch in range(NCH):
                        stop = (layer != 0) and (ch == last)
                        mm = t.matmul(p.psB[:, phalf * 512:phalf * 512 + 512],
                                      p.warena[:, fj * 1024 + ch * 128: fj * 1024 + (ch + 1) * 128],
                                      p.aarena[:, XH2 + ch * 2048 + u0: XH2 + ch * 2048 + u0 + 512],
                                      start=False, stop=stop)
                    if layer == 0:
                        mm = t.matmul(p.psB[:, phalf * 512:phalf * 512 + 512],
                                      p.warena[0:16, 20480 + fj * 128:20480 + (fj + 1) * 128],
                                      p.aarena[0:16, U1 + u0:U1 + u0 + 512],
                                      start=False, stop=True)
                    stn.bump(mm)
                p.block.tensor(t_fc)

            def s_gelh(s, half=half, u0=u0, fcg_base=fcg_base):
                for fj in range(16):
                    s.wait_ge(stn.sem, fcg_base + fj + 1)
                    s.activation(out=p.aarena[0:128, HS + fj * 512:HS + (fj + 1) * 512],
                                 in_=p.psB[:, (fj % 2) * 512:(fj % 2) * 512 + 512],
                                 func=AF.Gelu, bias=0.0,
                                 scale=(p.small[:, 32:33] if layer == 0 else 1.0))
                    s.drain().then_inc(ss.sem, 1)
            p.block.scalar(s_gelh)
            ss.v += 16
            t_hdone = ss.tok()

            if layer == 0:
                def t_u2(t, half=half, u0=u0, t_hdone=t_hdone):
                    waitall(t, [t_hdone])
                    for fj in range(16):
                        ch = 16 * half + fj
                        mm = t.matmul(p.psD[0:16, 0:512],
                                      p.aarena[:, AB + 128 + ch * 16:AB + 128 + (ch + 1) * 16],
                                      p.aarena[:, HS + fj * 512:HS + (fj + 1) * 512],
                                      start=(fj == 0), stop=(fj == 15))
                    stn.bump(mm)
                p.block.tensor(t_u2)
                t_u2t = stn.tok()

                def v_u2(v, half=half, u0=u0, t_u2t=t_u2t):
                    waitall(v, [t_u2t])
                    dstc = (U2A if half == 0 else U2B) + (u0 if half == 0 else 0)
                    v.tensor_scalar(out=p.aarena[0:16, dstc:dstc + 512],
                                    in0=p.psD[0:16, 0:512],
                                    scalar1=p.small[0:16, 41:42], scalar2=None,
                                    op0=ALU.mult)
                    sv.bump(v.drain())
                p.block.vector(v_u2)
            t_u2d = sv.tok()

            for oc in range(8):
                slot = oc % 2

                def sy_pj(sy, half=half, oc=oc, slot=slot):
                    sy.wait_ge(stn.sem, stn.v)
                    src = wq_dram(p, pjw, oc)[:, 16 * 128 * half:16 * 128 * half + 2048]
                    ld.bump(sy.dma_start(out=p.warena[:, 16384 + slot * 2048:16384 + (slot + 1) * 2048],
                                         in_=src))
                p.block.sync(sy_pj)
                t_pjl = ld.tok()

                def t_pj(t, half=half, un=un, u0=u0, oc=oc, slot=slot, t_pjl=t_pjl,
                         t_u2d=t_u2d, t_hdone=t_hdone):
                    waitall(t, [t_pjl, t_u2d, t_hdone])
                    if oc >= 2:
                        t.wait_ge(sv.sem, sv.v)
                    phalf = oc % 2
                    last = 15
                    for fj in range(16):
                        stop = (layer != 0 or half == 0) and (fj == last)
                        mm = t.matmul(p.psB[:, phalf * 512:phalf * 512 + 512],
                                      p.warena[:, 16384 + slot * 2048 + fj * 128:
                                                16384 + slot * 2048 + (fj + 1) * 128],
                                      p.aarena[:, HS + fj * 512:HS + (fj + 1) * 512],
                                      start=(fj == 0), stop=stop)
                    if layer == 0 and half == 1:
                        t.matmul(p.psB[:, phalf * 512:phalf * 512 + 512],
                                 p.warena[0:16, 22528 + oc * 128:22528 + (oc + 1) * 128],
                                 p.aarena[0:16, U2A + u0:U2A + u0 + 512],
                                 start=False, stop=False)
                        mm = t.matmul(p.psB[:, phalf * 512:phalf * 512 + 512],
                                      p.warena[0:16, 22528 + oc * 128:22528 + (oc + 1) * 128],
                                      p.aarena[0:16, U2B:U2B + 512],
                                      start=False, stop=True)
                    stn.bump(mm)
                p.block.tensor(t_pj)
                t_pjm = stn.tok()

                def v_px(v, half=half, un=un, u0=u0, oc=oc, t_pjm=t_pjm):
                    waitall(v, [t_pjm])
                    if half == 0 and un == 0 and oc == 0:
                        waitall(v, [t_badd])
                    phalf = oc % 2
                    if layer == 0:
                        v.scalar_tensor_tensor(
                            out=p.xs[:, oc * TOK + u0: oc * TOK + u0 + 512],
                            in0=p.psB[:, phalf * 512:phalf * 512 + 512],
                            scalar=p.small[:, 33:34],
                            in1=p.xs[:, oc * TOK + u0: oc * TOK + u0 + 512],
                            op0=ALU.mult, op1=ALU.add)
                    else:
                        v.tensor_tensor(
                            out=p.xs[:, oc * TOK + u0: oc * TOK + u0 + 512],
                            in0=p.xs[:, oc * TOK + u0: oc * TOK + u0 + 512],
                            in1=p.psB[:, phalf * 512:phalf * 512 + 512], op=ALU.add)
                    sv.bump(v.drain())
                p.block.vector(v_px)


ROLE_CHUNKS = {0: (0, 1, 6, 7), 1: (2, 3, 4, 5)}
def _l1_local(role, lc):
    return 512 * lc


def emit_l1_attn(p):
    io, ld, gd = p.io, p.s_ld, p.s_gd
    stn, sv, ss, scc = p.s_t, p.s_v, p.s_s, p.s_cc

    XH3 = 0
    LAT = 16384
    KDUP = 16384
    VK = 20480
    PT = 22560
    YC = 23584
    ODD = 27680
    KVO = 27680
    VKO = 22560
    MB = 4352

    p.LN_XB = 16384
    p.LN_SQ = 20480
    t_ln = emit_ln(p, [("x", 0, 1024), ("x", 1024, 1024)], XH3, pre_toks=())

    def g_w1(g):
        g.wait_ge(stn.sem, stn.v)
        gd.bump(g.dma_start(out=p.warena[:, 0:4096], in_=io["Wcd_t"][:, :]))
        gd.bump(g.dma_start(out=p.warena[:, 4096:8192], in_=io["Wqu_t"][:, :]))
        gd.bump(g.dma_start(out=p.warena[:, 8192:8704], in_=io["Wkv_t"][:, :]))
        gd.bump(g.dma_start(out=p.warena[:, 8704:16896], in_=io["Wcp_t"][:, :]))
        g.wait_ge(gd.sem, gd.v)
    p.block.gpsimd(g_w1)
    t_w = gd.tok()

    def g_mask(g):
        g.wait_ge(sv.sem, sv.v)
        g.memset(p.consts[:, MB:MB + 1024], 1.0)
        g.drain()
        g.affine_select(out=p.consts[:, MB:MB + 1024], in_=p.consts[:, MB:MB + 1024],
                        pattern=[[1, 1024]], compare_op=ALU.is_ge, fill=0.0,
                        base=-384, channel_multiplier=-1)
        p.s_g.bump(g.drain())
    p.block.gpsimd(g_mask)
    t_mk = p.s_g.tok()

    for lc in range(4):
        for un in range(4):
            half = (4 * lc + un) % 2

            def t_lat(t, lc=lc, un=un, half=half):
                if lc == 0 and un == 0:
                    waitall(t, [t_ln, t_w])
                t.wait_ge(ss.sem, ss.v)
                for ch in range(NCH):
                    mm = t.matmul(p.psB[:, half * 512:half * 512 + 512],
                                  p.warena[:, lc * 1024 + ch * 128: lc * 1024 + (ch + 1) * 128],
                                  p.aarena[:, XH3 + ch * 2048 + un * 512: XH3 + ch * 2048 + (un + 1) * 512],
                                  start=(ch == 0), stop=(ch == NCH - 1))
                stn.bump(mm)
            p.block.tensor(t_lat)
            t_lm = stn.tok()

            def s_le(s, lc=lc, un=un, half=half, t_lm=t_lm):
                waitall(s, [t_lm])
                s.activation(out=p.aarena[:, LAT + lc * 2048 + un * 512: LAT + lc * 2048 + (un + 1) * 512],
                             in_=p.psB[:, half * 512:half * 512 + 512],
                             func=AF.Identity, bias=0.0, scale=1.0)
                ss.bump(s.drain())
            p.block.scalar(s_le)
    t_lat_d = ss.tok()

    for qc in range(8):
        for un in range(4):
            half = (4 * qc + un) % 2

            def t_q(t, qc=qc, un=un, half=half):
                if qc == 0 and un == 0:
                    waitall(t, [t_lat_d])
                t.wait_ge(ss.sem, ss.v)
                for ch in range(4):
                    mm = t.matmul(p.psB[:, half * 512:half * 512 + 512],
                                  p.warena[:, 4096 + qc * 512 + ch * 128: 4096 + qc * 512 + (ch + 1) * 128],
                                  p.aarena[:, LAT + ch * 2048 + un * 512: LAT + ch * 2048 + (un + 1) * 512],
                                  start=(ch == 0), stop=(ch == 3))
                stn.bump(mm)
            p.block.tensor(t_q)
            t_qm = stn.tok()

            def s_qe(s, qc=qc, un=un, half=half, t_qm=t_qm):
                waitall(s, [t_qm])
                s.activation(out=p.aarena[:, XH3 + qc * 2048 + un * 512: XH3 + qc * 2048 + (un + 1) * 512],
                             in_=p.psB[:, half * 512:half * 512 + 512],
                             func=AF.Identity, bias=0.0, scale=0.125)
                ss.bump(s.drain())
            p.block.scalar(s_qe)

    for un in range(4):
        def t_k(t, un=un):
            t.wait_ge(ss.sem, ss.v)
            for ch in range(4):
                mm = t.matmul(p.psD[0:64, 0:512],
                              p.warena[:, 8192 + ch * 128: 8192 + ch * 128 + 64],
                              p.aarena[:, LAT + ch * 2048 + un * 512: LAT + ch * 2048 + (un + 1) * 512],
                              start=(ch == 0), stop=(ch == 3))
            stn.bump(mm)
        p.block.tensor(t_k)
        t_km = stn.tok()

        def s_kev(s, un=un, t_km=t_km):
            waitall(s, [t_km])
            s.activation(out=p.aarena[0:64, KVO + un * 512: KVO + (un + 1) * 512],
                         in_=p.psD[0:64, 0:512], func=AF.Identity, bias=0.0, scale=1.0)
            ss.bump(s.drain())
        p.block.scalar(s_kev)
    for kt in range(16):
        def t_v(t, kt=kt):
            t.wait_ge(sv.sem, sv.v)
            t.wait_ge(ss.sem, ss.v)
            for ch in range(4):
                mm = t.matmul(p.psD[:, 0:64],
                              p.aarena[:, LAT + ch * 2048 + kt * 128: LAT + ch * 2048 + (kt + 1) * 128],
                              p.warena[:, 8192 + ch * 128 + 64: 8192 + ch * 128 + 128],
                              start=(ch == 0), stop=(ch == 3))
            stn.bump(mm)
        p.block.tensor(t_v)
        t_vm = stn.tok()

        def v_vev(v, kt=kt, t_vm=t_vm):
            waitall(v, [t_vm])
            v.tensor_copy(out=p.aarena[:, VKO + kt * 64: VKO + (kt + 1) * 64], in_=p.psD[:, 0:64])
            sv.bump(v.drain())
        p.block.vector(v_vev)
    t_kv = sv.tok()

    def g_kex(g):
        waitall(g, [t_kv])
        g.wait_ge(ss.sem, ss.v)
        gd.bump(g.dma_start(out=p.cc_kin[0:131072].rearrange("(q c) -> q c", q=64),
                            in_=p.aarena[0:64, KVO:KVO + 2048]))
        gd.bump(g.dma_start(out=p.cc_kin[131072:262144].rearrange("(q c) -> q c", q=128),
                            in_=p.aarena[:, VKO:VKO + 1024]))
        g.wait_ge(gd.sem, gd.v)
        scc.bump(g.collective_compute(
            "AllGather", ALU.bypass,
            replica_groups=[[0, 1], [2, 3], [4, 5], [6, 7]],
            ins=[p.cc_kin[:]], outs=[p.cc_kout[:]]))
        g.wait_ge(scc.sem, scc.v)
        for (slab, src0, w, dst0) in ((0, 0, 1024, 0), (0, 1024, 1024, 3072),
                                      (1, 0, 2048, 1024)):
            src = p.cc_kout[262144 * slab + src0:262144 * slab + src0 + 0 + 131072]                 [0:0] if False else None
            kbase = 262144 * slab
            srcap = p.cc_kout[kbase:kbase + 131072].rearrange("(q c) -> q c", q=64)[:, src0:src0 + w]
            gd.bump(g.dma_start(out=p.aarena[0:64, KDUP + dst0:KDUP + dst0 + w], in_=srcap))
            gd.bump(g.dma_start(out=p.aarena[64:128, KDUP + dst0:KDUP + dst0 + w], in_=srcap))
        for (slab, t0, nkt, gkt0) in ((0, 0, 8, 0), (0, 8, 8, 24), (1, 0, 16, 8)):
            kbase = 262144 * slab + 131072
            srcap = p.cc_kout[kbase:kbase + 131072].rearrange("(q k c) -> q k c", q=128, k=16)                 [:, t0:t0 + nkt, :]
            dst = p.aarena[:, VK + gkt0 * 65: VK + (gkt0 + nkt) * 65]                 .rearrange("q (k c) -> q k c", c=65)[:, :, 0:64]
            gd.bump(g.dma_start(out=dst, in_=srcap))
        g.wait_ge(gd.sem, gd.v)
    p.block.gpsimd(g_kex)
    t_kx = gd.tok()

    def v_vone(v):
        waitall(v, [t_kx])
        v.memset(p.aarena[:, VK:VK + 32 * 65].rearrange("q (k c) -> q k c", c=65)[:, :, 64:65], 1.0)
        sv.bump(v.drain())
    p.block.vector(v_vone)
    t_vo = sv.tok()

    for lc in range(4):
        u0 = 512 * lc
        nk = {0: 4 * ROLE_CHUNKS[0][lc] + 4, 1: 4 * ROLE_CHUNKS[1][lc] + 4}
        nmax = max(nk[0], nk[1])
        dg = {r: 4 * ROLE_CHUNKS[r][lc] for r in (0, 1)}

        for hp in range(8):
            p.pa_mm_base = stn.v
            p.pa_red_base = sv.v

            def t_pa(t, lc=lc, hp=hp, u0=u0):
                if lc == 0 and hp == 0:
                    waitall(t, [t_vo, t_mk])
                    t.wait_ge(ss.sem, ss.v)
                with t.register(f"r1a{lc}_{hp}") as r:
                    t.reg_load(r, p.rolei[0:1, 0:1])
                    for role in (0, 1):
                        ctx = t.If_eq(r, 0) if role == 0 else t.Else()
                        with ctx:
                            for kt in range(nk[role]):
                                pa = p.psA if kt % 2 == 0 else p.psB
                                if kt >= 2:
                                    t.wait_ge(sv.sem, p.pa_red_base + kt - 1)
                                elif kt == 0:
                                    t.wait_ge(sv.sem, p.pa_red_base)
                                t.matmul(pa[:, 0:512],
                                         p.aarena[0:64, KDUP + kt * 128: KDUP + (kt + 1) * 128],
                                         p.aarena[0:64, XH3 + hp * 2048 + u0: XH3 + hp * 2048 + u0 + 512],
                                         start=True, stop=True, tile_position=(0, 0))
                                t.matmul(pa[:, 512:1024],
                                         p.aarena[64:128, KDUP + kt * 128: KDUP + (kt + 1) * 128],
                                         p.aarena[64:128, XH3 + hp * 2048 + u0: XH3 + hp * 2048 + u0 + 512],
                                         start=True, stop=True, tile_position=(64, 0)
                                         ).then_inc(stn.sem, 1)
                            if nk[role] < nmax:
                                t.wait_ge(stn.sem, p.pa_mm_base + nk[role])
                            for _ in range(nmax - nk[role]):
                                t.sem_inc(stn.sem, 1)
                    t.end_ifs()
            p.block.tensor(t_pa)
            stn.v += nmax

            def v_red(v, lc=lc, hp=hp):
                with v.register(f"r1b{lc}_{hp}") as r:
                    v.reg_load(r, p.rolei[0:1, 0:1])
                    for role in (0, 1):
                        ctx = v.If_eq(r, 0) if role == 0 else v.Else()
                        with ctx:
                            for kt in range(nk[role]):
                                pa = p.psA if kt % 2 == 0 else p.psB
                                v.wait_ge(stn.sem, p.pa_mm_base + kt + 1)
                                v.reduce_max(out=p.fsc[:, 256 + hp * 32 + kt:257 + hp * 32 + kt],
                                             in_=pa[:, 0:1024], axis=AX.X)
                                v.drain().then_inc(sv.sem, 1)
                            v.reduce_max(out=p.small[:, 80 + hp:81 + hp],
                                         in_=p.fsc[:, 256 + hp * 32:256 + hp * 32 + nk[role]],
                                         axis=AX.X)
                            if nk[role] < nmax:
                                v.wait_ge(sv.sem, p.pa_red_base + nk[role])
                            for _ in range(nmax - nk[role]):
                                v.sem_inc(sv.sem, 1)
                            v.drain().then_inc(sv.sem, 1)
                    v.end_ifs()
            p.block.vector(v_red)
            sv.v += nmax + 1

        t_pamax = sv.tok()
        emit_shift(p, t_pamax)
        t_bias = p.t_shift

        for hp in range(8):
            def roles_pb(eng, emit):
                pass

            p.pb_mm_base = stn.v
            p.pb_exp_base = ss.v
            p.pb_msk_base = sv.v

            def t_pb(t, lc=lc, hp=hp, u0=u0):
                if hp == 0:
                    waitall(t, [t_bias])
                with t.register(f"r1c{lc}_{hp}") as r:
                    t.reg_load(r, p.rolei[0:1, 0:1])
                    for role in (0, 1):
                        ctx = t.If_eq(r, 0) if role == 0 else t.Else()
                        with ctx:
                            for kt in range(nk[role]):
                                if kt == 0:
                                    t.wait_ge(sv.sem, p.pb_msk_base)
                                    t.wait_ge(ss.sem, p.pb_exp_base)
                                else:
                                    t.wait_ge(ss.sem, p.pb_exp_base + kt)
                                    if kt - 1 >= dg[role]:
                                        t.wait_ge(sv.sem, p.pb_msk_base + kt - dg[role])
                                t.matmul(p.psA[:, 0:512],
                                         p.aarena[0:64, KDUP + kt * 128: KDUP + (kt + 1) * 128],
                                         p.aarena[0:64, XH3 + hp * 2048 + u0: XH3 + hp * 2048 + u0 + 512],
                                         start=True, stop=True, tile_position=(0, 0))
                                t.matmul(p.psA[:, 512:1024],
                                         p.aarena[64:128, KDUP + kt * 128: KDUP + (kt + 1) * 128],
                                         p.aarena[64:128, XH3 + hp * 2048 + u0: XH3 + hp * 2048 + u0 + 512],
                                         start=True, stop=True, tile_position=(64, 0)
                                         ).then_inc(stn.sem, 1)
                                if kt >= dg[role]:
                                    t.wait_ge(sv.sem, p.pb_msk_base + kt - dg[role] + 1)
                                else:
                                    t.wait_ge(ss.sem, p.pb_exp_base + kt + 1)
                                t.matmul(p.psC[0:65, 0:512],
                                         p.aarena[:, VK + kt * 65: VK + (kt + 1) * 65],
                                         p.aarena[:, PT:PT + 512],
                                         start=(kt == 0), stop=(kt == nk[role] - 1))
                                t.matmul(p.psC[0:65, 512:1024],
                                         p.aarena[:, VK + kt * 65: VK + (kt + 1) * 65],
                                         p.aarena[:, PT + 512:PT + 1024],
                                         start=(kt == 0), stop=(kt == nk[role] - 1)
                                         ).then_inc(stn.sem, 1)
                            if nk[role] < nmax:
                                t.wait_ge(stn.sem, p.pb_mm_base + 2 * nk[role])
                            for _ in range(2 * (nmax - nk[role])):
                                t.sem_inc(stn.sem, 1)
                    t.end_ifs()
            p.block.tensor(t_pb)
            stn.v += 2 * nmax

            def s_exp(s, lc=lc, hp=hp):
                with s.register(f"r1d{lc}_{hp}") as r:
                    s.reg_load(r, p.rolei[0:1, 0:1])
                    for role in (0, 1):
                        ctx = s.If_eq(r, 0) if role == 0 else s.Else()
                        with ctx:
                            for kt in range(nk[role]):
                                s.wait_ge(stn.sem, p.pb_mm_base + 2 * kt + 1)
                                s.activation(out=p.aarena[:, PT:PT + 1024], in_=p.psA[:, 0:1024],
                                             func=AF.Exp, bias=p.small[:, 96 + hp:97 + hp],
                                             scale=1.0)
                                s.drain().then_inc(ss.sem, 1)
                            if nk[role] < nmax:
                                s.wait_ge(ss.sem, p.pb_exp_base + nk[role])
                            for _ in range(nmax - nk[role]):
                                s.sem_inc(ss.sem, 1)
                    s.end_ifs()
            p.block.scalar(s_exp)
            ss.v += nmax

            def v_msk(v, lc=lc, hp=hp):
                with v.register(f"r1e{lc}_{hp}") as r:
                    v.reg_load(r, p.rolei[0:1, 0:1])
                    for role in (0, 1):
                        ctx = v.If_eq(r, 0) if role == 0 else v.Else()
                        with ctx:
                            for j in range(4):
                                kt = dg[role] + j
                                v.wait_ge(ss.sem, p.pb_exp_base + kt + 1)
                                m = p.consts[:, MB + 384 - 128 * j:MB + 896 - 128 * j]
                                v.tensor_tensor(out=p.aarena[:, PT:PT + 512],
                                                in0=p.aarena[:, PT:PT + 512], in1=m, op=ALU.mult)
                                v.tensor_tensor(out=p.aarena[:, PT + 512:PT + 1024],
                                                in0=p.aarena[:, PT + 512:PT + 1024], in1=m,
                                                op=ALU.mult)
                                v.drain().then_inc(sv.sem, 1)
                    v.end_ifs()
            p.block.vector(v_msk)
            sv.v += 4
            t_avd = stn.tok()

            def v_lrec(v, hp=hp, t_avd=t_avd):
                waitall(v, [t_avd])
                v.tensor_copy(out=p.fsc[64:65, 0:1024], in_=p.psC[64:65, 0:1024])
                v.drain()
                v.reciprocal(out=p.fsc[64:65, 0:1024], in_=p.fsc[64:65, 0:1024])
                sv.bump(v.drain())
            p.block.vector(v_lrec)
            t_rb = sv.tok()

            def t_rbc(t, t_rb=t_rb):
                waitall(t, [t_rb])
                t.matmul(p.psD[0:64, 0:512], p.onesr[64:65, 0:64],
                         p.fsc[64:65, 0:512], start=True, stop=True)
                stn.bump(t.matmul(p.psB[0:64, 0:512], p.onesr[64:65, 0:64],
                                  p.fsc[64:65, 512:1024], start=True, stop=True))
            p.block.tensor(t_rbc)
            t_rp = stn.tok()

            def v_y(v, hp=hp, t_rp=t_rp, gdw=gd.v):
                waitall(v, [t_rp])
                if hp % 4 == 0:
                    v.wait_ge(gd.sem, gdw)
                v.tensor_copy(out=p.aarena[0:64, PT:PT + 512], in_=p.psD[0:64, 0:512])
                v.tensor_copy(out=p.aarena[0:64, PT + 512:PT + 1024], in_=p.psB[0:64, 0:512])
                v.drain()
                v.tensor_tensor(out=p.aarena[0:64, YC + hp * 512:YC + (hp + 1) * 512],
                                in0=p.psC[0:64, 0:512], in1=p.aarena[0:64, PT:PT + 512],
                                op=ALU.mult)
                v.tensor_tensor(out=p.aarena[0:64, ODD + (hp % 4) * 512:ODD + (hp % 4 + 1) * 512],
                                in0=p.psC[0:64, 512:1024], in1=p.aarena[0:64, PT + 512:PT + 1024],
                                op=ALU.mult)
                sv.bump(v.drain())
            p.block.vector(v_y)
            t_yh = sv.tok()

            if hp % 4 == 3:
                def g_yodd(g, hp=hp, t_yh=t_yh):
                    waitall(g, [t_yh])
                    src = p.aarena[0:64, ODD:ODD + 2048].rearrange("q (k c) -> q k c", k=4)
                    dst = p.aarena[64:128, YC + (hp - 3) * 512:YC + (hp + 1) * 512]                         .rearrange("q (k c) -> q k c", k=4)
                    gd.bump(g.dma_start(out=dst, in_=src))
                    g.wait_ge(gd.sem, gd.v)
                p.block.gpsimd(g_yodd)
        t_yall = gd.tok()

        for oc in range(8):
            half = oc % 2

            def t_o(t, lc=lc, u0=u0, oc=oc, half=half, t_yall=t_yall):
                if oc == 0:
                    waitall(t, [t_yall])
                if oc >= 2:
                    t.wait_ge(sv.sem, sv.v)
                for ch in range(NCH):
                    mm = t.matmul(p.psB[:, half * 512:half * 512 + 512],
                                  p.warena[:, 8704 + oc * 1024 + ch * 128: 8704 + oc * 1024 + (ch + 1) * 128],
                                  p.aarena[:, YC + ch * 512: YC + (ch + 1) * 512],
                                  start=(ch == 0), stop=(ch == NCH - 1))
                stn.bump(mm)
            p.block.tensor(t_o)
            t_om = stn.tok()

            def v_res(v, lc=lc, u0=u0, oc=oc, half=half, t_om=t_om):
                waitall(v, [t_om])
                v.tensor_tensor(out=p.xs[:, oc * TOK + u0: oc * TOK + u0 + 512],
                                in0=p.xs[:, oc * TOK + u0: oc * TOK + u0 + 512],
                                in1=p.psB[:, half * 512:half * 512 + 512], op=ALU.add)
                sv.bump(v.drain())
            p.block.vector(v_res)


def emit_shift(p, t_pamax):
    stn, sv, ss = p.s_t, p.s_v, p.s_s

    def g_smax(g):
        waitall(g, [t_pamax])
        g.reduce_max(out=p.small[0:1, 89:97], in_=p.small[:, 80:88], axis=AX.C)
        p.s_g.bump(g.drain())
    p.block.gpsimd(g_smax)
    t_r2 = p.s_g.tok()

    def t_b(t):
        waitall(t, [t_r2])
        stn.bump(t.matmul(p.psD[:, 0:8], p.onesr[0:1, 0:128], p.small[0:1, 89:97],
                          start=True, stop=True))
    p.block.tensor(t_b)
    t_bt = stn.tok()

    def s_n(s):
        waitall(s, [t_bt])
        s.activation(out=p.small[:, 96:104], in_=p.psD[:, 0:8],
                     func=AF.Identity, bias=0.0, scale=-1.0)
        ss.bump(s.drain())
    p.block.scalar(s_n)
    p.t_shift = ss.tok()


def emit_l1_ffn(p):
    emit_ffn(p, layer=1)


def kernel(**inputs):
    per_core = stage_inputs(inputs)
    nc = build_program()
    res = run_bass_kernel_spmd(nc, per_core, core_ids=list(range(NCORES)))
    return gather_output(res)
